# version 04 (final)
# baseline (speedup 1.0000x reference)
import contextlib
import numpy as np
import concourse.bass as bass
import concourse.mybir as mybir
from concourse.bass_utils import run_bass_kernel_spmd

F32 = mybir.dt.float32
BF16 = mybir.dt.bfloat16
I32 = mybir.dt.int32
AF = mybir.ActivationFunctionType
ALU = mybir.AluOpType

D = 2048
TPC = 4096
DC = D // 128
NT = TPC // 512
EPS = 1e-5
ENGINES = ("sync", "tensor", "vector", "scalar", "gpsimd")


class Prog:

    def __init__(self, nc, block, sems, cnt=None, nbar=0):
        self.nc = nc
        self.block = block
        self.sems = sems
        self.cnt = dict(cnt) if cnt else {k: 0 for k in sems}
        self.nbar = nbar

    def inc(self, name, amt=1):
        self.cnt[name] += amt
        return self.cnt[name]

    def val(self, name):
        return self.cnt[name]

    def on(self, engine):
        return getattr(self.block, engine)

    def w(self, e, *names):
        for n in names:
            e.wait_ge(self.sems[n], self.cnt[n])

    def wv(self, e, name, value):
        if value > 0:
            e.wait_ge(self.sems[name], value)

    BARRIER_WAITS = {"sync": ("ld", "st"), "tensor": ("te",), "vector": ("ve",),
                     "scalar": ("se", "st"), "gpsimd": ("gd", "st")}

    def barrier(self):
        self.nbar += 1
        tgt = self.nbar * len(ENGINES)
        for name in ENGINES:
            @self.on(name)
            def _(e, name=name):
                self.w(e, *self.BARRIER_WAITS[name])
                e.sem_inc(self.sems["bar"], 1)
                e.wait_ge(self.sems["bar"], tgt)


def make_sems(nc, stack):
    names = ["ld", "gd", "st", "te", "ve", "se", "bar"]
    return {n: stack.enter_context(nc.semaphore(n)) for n in names}


def emit_ln(p, x_hbm, ln_sb, ones_col, ones_row1, addend_hbm=None, xnew_hbm=None,
            negcs=None, tag=""):
    nc, S = p.nc, p.sems
    with contextlib.ExitStack() as st_:
        sb = lambda name, shape, dt: st_.enter_context(nc.sbuf_tensor(tag + name, shape, dt))
        ps = lambda name, shape: st_.enter_context(nc.psum_tensor(tag + name, shape, F32))
        xt = sb("xt", [128, TPC // 4], F32)
        adt = sb("adt", [128, TPC // 4], F32) if addend_hbm is not None else None
        xsq = sb("xsq", [128, 512], BF16)
        vA = sb("vA", [1, TPC], F32)
        vB = sb("vB", [1, TPC], F32)
        vC = sb("vC", [1, TPC], F32)
        bc = sb("bc", [128, TPC], BF16)
        cst = sb("cst", [1, 2], F32)
        st0 = ps("st0", [1, 512])
        st1 = ps("st1", [1, 512])
        bps = ps("bps", [128, 512])

        @p.on("vector")
        def _(e):
            e.memset(cst[:, 0:1], EPS)
            e.memset(cst[:, 1:2], -0.5).then_inc(S["ve"], 1)
            p.inc("ve", 1)

        HW = TPC // 4
        for c in range(DC):
            for hf in range(4):
                sl = slice(hf * HW, (hf + 1) * HW)
                @p.on("sync")
                def _(e, c=c, sl=sl):
                    if c >= 1 or sl.start > 0:
                        p.w(e, "ve")
                        if addend_hbm is not None:
                            p.w(e, "st")
                    e.dma_start(out=xt[:], in_=x_hbm[c * 128:(c + 1) * 128, sl]).then_inc(S["ld"], 16)
                    p.inc("ld", 16)
                    if addend_hbm is not None:
                        e.dma_start(out=adt[:], in_=addend_hbm[c * 128:(c + 1) * 128, sl]).then_inc(S["ld"], 16)
                        p.inc("ld", 16)

                @p.on("vector")
                def _(e, c=c, sl=sl):
                    p.w(e, "ld")
                    if addend_hbm is not None:
                        e.tensor_add(xt[:], xt[:], adt[:]).then_inc(S["ve"], 1)
                        p.inc("ve", 1)
                        p.w(e, "ve")
                    e.tensor_copy(ln_sb[c][:, sl], xt[:]).then_inc(S["ve"], 1)
                    p.inc("ve", 1)

                if addend_hbm is not None:
                    @p.on("sync")
                    def _(e, c=c, sl=sl):
                        p.w(e, "ve")
                        e.dma_start(out=xnew_hbm[c * 128:(c + 1) * 128, sl], in_=xt[:]).then_inc(S["st"], 16)
                        p.inc("st", 16)

        p.barrier()

        for t in range(NT):
            for c in range(DC):
                @p.on("scalar")
                def _(e, c=c, t=t):
                    p.w(e, "te")
                    if c == 0:
                        p.w(e, "ve")
                    e.activation(xsq[:], ln_sb[c][:, t * 512:(t + 1) * 512], AF.Square).then_inc(S["se"], 1)
                    p.inc("se", 1)

                @p.on("tensor")
                def _(e, c=c, t=t):
                    p.w(e, "se")
                    e.matmul(st0[:], ones_col[:], ln_sb[c][:, t * 512:(t + 1) * 512],
                             start=(c == 0), stop=(c == DC - 1))
                    e.matmul(st1[:], ones_col[:], xsq[:], start=(c == 0),
                             stop=(c == DC - 1)).then_inc(S["te"], 1)
                    p.inc("te", 1)

            @p.on("vector")
            def _(e, t=t):
                p.w(e, "te")
                e.tensor_copy(vA[:, t * 512:(t + 1) * 512], st0[:])
                e.tensor_copy(vB[:, t * 512:(t + 1) * 512], st1[:]).then_inc(S["ve"], 1)
                p.inc("ve", 1)

        p.barrier()

        @p.on("vector")
        def _(e):
            p.w(e, "ve")
            e.tensor_scalar_mul(vC[:], vA[:], 1.0 / D).then_inc(S["ve"], 1)
            p.inc("ve", 1)
            e.tensor_scalar_mul(vB[:], vB[:], 1.0 / D).then_inc(S["ve"], 1)
            p.inc("ve", 1)
            p.w(e, "ve")
            e.tensor_mul(vA[:], vC[:], vC[:]).then_inc(S["ve"], 1)
            p.inc("ve", 1)
            p.w(e, "ve")
            e.tensor_sub(vB[:], vB[:], vA[:]).then_inc(S["ve"], 1)
            p.inc("ve", 1)

        @p.on("scalar")
        def _(e):
            p.w(e, "ve")
            e.activation(vA[:], vB[:], AF.Ln, bias=cst[:, 0:1]).then_inc(S["se"], 1)
            p.inc("se", 1)
            p.w(e, "se")
            e.activation(vB[:], vA[:], AF.Exp, scale=cst[:, 1:2]).then_inc(S["se"], 1)
            p.inc("se", 1)

        @p.on("vector")
        def _(e):
            p.w(e, "se", "ve")
            e.tensor_mul(vA[:], vC[:], vB[:]).then_inc(S["ve"], 1)
            p.inc("ve", 1)

        for src, op in ((vB, "mul"), (vA, "sub")):
            for t in range(NT):
                @p.on("tensor")
                def _(e, t=t, src=src):
                    p.w(e, "ve")
                    e.matmul(bps[:], ones_row1[:], src[:, t * 512:(t + 1) * 512],
                             start=True, stop=True).then_inc(S["te"], 1)
                    p.inc("te", 1)

                @p.on("vector")
                def _(e, t=t):
                    p.w(e, "te")
                    e.tensor_copy(bc[:, t * 512:(t + 1) * 512], bps[:]).then_inc(S["ve"], 1)
                    p.inc("ve", 1)

            for c in range(DC):
                @p.on("vector")
                def _(e, c=c, op=op):
                    if c == 0:
                        p.w(e, "ve")
                    if op == "mul":
                        inst = e.tensor_mul(ln_sb[c][:], ln_sb[c][:], bc[:])
                    else:
                        inst = e.tensor_sub(ln_sb[c][:], ln_sb[c][:], bc[:])
                    if c == DC - 1:
                        inst.then_inc(S["ve"], 1)
                        p.inc("ve", 1)

            p.barrier()

        if negcs is not None:
            for t in range(NT):
                @p.on("tensor")
                def _(e, t=t):
                    if t == 0:
                        p.w(e, "ve")
                    else:
                        p.w(e, "ve")
                    for c in range(DC):
                        mm = e.matmul(st0[:], ones_col[:], ln_sb[c][:, t * 512:(t + 1) * 512],
                                      start=(c == 0), stop=(c == DC - 1))
                    mm.then_inc(S["te"], 1)
                    p.inc("te", 1)

                @p.on("vector")
                def _(e, t=t):
                    p.w(e, "te")
                    e.tensor_scalar_mul(negcs[:, t * 512:(t + 1) * 512], st0[:], -128.0).then_inc(S["ve"], 1)
                    p.inc("ve", 1)
    p.barrier()


def build_l0ffn(with_addend):
    nc = bass.Bass("TRN2", target_bir_lowering=False)
    x = nc.dram_tensor("x", [D, TPC], F32, kind="ExternalInput").ap()
    wgT = nc.dram_tensor("wgT", [D, 8192], I32, kind="ExternalInput").ap()
    wuT = nc.dram_tensor("wuT", [D, 8192], I32, kind="ExternalInput").ap()
    wdT = nc.dram_tensor("wdT", [8192, D], I32, kind="ExternalInput").ap()
    sg = nc.dram_tensor("sg", [128, 64], F32, kind="ExternalInput").ap()
    su = nc.dram_tensor("su", [128, 64], F32, kind="ExternalInput").ap()
    sd = nc.dram_tensor("sd", [128, 16], F32, kind="ExternalInput").ap()
    xout = nc.dram_tensor("xout", [D, TPC], F32, kind="ExternalOutput").ap()
    if with_addend:
        addend = nc.dram_tensor("addend", [D, TPC], F32, kind="ExternalInput").ap()
        xsum = nc.dram_tensor("xsum", [D, TPC], F32, kind="Internal").ap()
    else:
        addend, xsum = None, None
    h_hbm = nc.dram_tensor("hscratch", [8192, TPC], BF16, kind="Internal").ap()

    with contextlib.ExitStack() as outer:
        sems = make_sems(nc, outer)
        sbo = lambda name, shape, dt: outer.enter_context(nc.sbuf_tensor(name, shape, dt))
        sgt = sbo("sgt", [128, 64], F32)
        sut = sbo("sut", [128, 64], F32)
        sdt2 = sbo("sdt2", [128, 16], F32)
        ones1b = sbo("ones1b", [1, 128], BF16)
        ones_col = sbo("ones_col", [128, 1], BF16)
        ones_row1 = sbo("ones_row1", [1, 128], F32)
        negcs = sbo("negcs", [1, TPC], BF16)

        with nc.Block() as block:
            p = Prog(nc, block, sems)

            @p.on("vector")
            def _(e):
                e.memset(ones_col[:], 1.0)
                e.memset(ones_row1[:], 1.0)
                e.memset(ones1b[:], 1.0).then_inc(sems["ve"], 1)
                p.inc("ve", 1)

            @p.on("sync")
            def _(e):
                for t_, s_ in ((sgt, sg), (sut, su), (sdt2, sd)):
                    e.dma_start(out=t_[:], in_=s_[:]).then_inc(sems["ld"], 16)
                    p.inc("ld", 16)

            @p.on("vector")
            def _(e):
                p.w(e, "ld")
                e.tensor_scalar_mul(sdt2[:], sdt2[:], 0.5).then_inc(sems["ve"], 1)
                p.inc("ve", 1)

            p.barrier()

            with contextlib.ExitStack() as ph1:
                sb1 = lambda name, shape, dt: ph1.enter_context(nc.sbuf_tensor(name, shape, dt))
                ln_sb = [sb1(f"ln{c}", [128, TPC], BF16) for c in range(DC)]

                emit_ln(p, x, ln_sb, ones_col, ones_row1, addend_hbm=addend,
                        xnew_hbm=xsum, negcs=negcs)

                with contextlib.ExitStack() as gsc:
                    sbg = lambda name, shape, dt: gsc.enter_context(nc.sbuf_tensor(name, shape, dt))
                    wg_sb = [sbg("wgb0", [128, D], BF16), sbg("wgb1", [128, D], BF16)]
                    wu_sb = [sbg("wub0", [128, D], BF16), sbg("wub1", [128, D], BF16)]
                    g_sb = [sbg(f"g_sb{j}", [128, 512], BF16) for j in range(2)]
                    u_sb = [sbg(f"u_sb{j}", [128, 512], BF16) for j in range(2)]
                    h_sb = [sbg(f"h_sb{j}", [128, 512], BF16) for j in range(2)]
                    pg = [gsc.enter_context(nc.psum_tensor(f"pg{j}", [128, 512], F32)) for j in range(2)]
                    pu = [gsc.enter_context(nc.psum_tensor(f"pu{j}", [128, 512], F32)) for j in range(2)]
                    ve_done = {}
                    se_done = {}
                    st_done = {}
                    te0 = p.val("te")

                    for i in range(64):
                        b = i % 2
                        @p.on("gpsimd")
                        def _(e, i=i, b=b):
                            if i >= 2:
                                p.wv(e, "te", te0 + (i - 1) * NT)
                            e.dma_start(out=wg_sb[b][:].rearrange("p (k m) -> p k m", m=128),
                                        in_=wgT[:, i * 128:(i + 1) * 128].rearrange(
                                            "(k p) m -> p k m", p=128)).then_inc(sems["gd"], 16)
                            e.dma_start(out=wu_sb[b][:].rearrange("p (k m) -> p k m", m=128),
                                        in_=wuT[:, i * 128:(i + 1) * 128].rearrange(
                                            "(k p) m -> p k m", p=128)).then_inc(sems["gd"], 16)
                            p.inc("gd", 32)

                        for t in range(NT):
                            it = i * NT + t
                            j = it % 2
                            @p.on("tensor")
                            def _(e, i=i, b=b, t=t, it=it, j=j):
                                if t == 0:
                                    p.w(e, "gd")
                                p.wv(e, "ve", ve_done.get(it - 2, 0))
                                for k in range(DC):
                                    e.matmul(pg[j][:], wg_sb[b][:, k * 128:(k + 1) * 128],
                                             ln_sb[k][:, t * 512:(t + 1) * 512],
                                             start=(k == 0), stop=False)
                                e.matmul(pg[j][:], ones1b[:], negcs[:, t * 512:(t + 1) * 512],
                                         start=False, stop=True)
                                for k in range(DC):
                                    e.matmul(pu[j][:], wu_sb[b][:, k * 128:(k + 1) * 128],
                                             ln_sb[k][:, t * 512:(t + 1) * 512],
                                             start=(k == 0), stop=False)
                                e.matmul(pu[j][:], ones1b[:], negcs[:, t * 512:(t + 1) * 512],
                                         start=False, stop=True).then_inc(sems["te"], 1)
                                p.inc("te", 1)

                            @p.on("scalar")
                            def _(e, i=i, t=t, it=it, j=j):
                                p.w(e, "te")
                                p.wv(e, "ve", ve_done.get(it - 2, 0))
                                e.activation(g_sb[j][:], pg[j][:], AF.Silu, scale=sgt[:, i:i + 1]).then_inc(sems["se"], 1)
                                p.inc("se", 1)
                                se_done[it] = p.val("se")

                            @p.on("vector")
                            def _(e, i=i, t=t, it=it, j=j):
                                p.wv(e, "se", se_done[it])
                                p.wv(e, "st", st_done.get(it - 2, 0))
                                e.tensor_scalar_mul(u_sb[j][:], pu[j][:], sut[:, i:i + 1]).then_inc(sems["ve"], 1)
                                p.inc("ve", 1)
                                p.w(e, "ve")
                                e.tensor_mul(h_sb[j][:], g_sb[j][:], u_sb[j][:]).then_inc(sems["ve"], 1)
                                p.inc("ve", 1)
                                ve_done[it] = p.val("ve")

                            @p.on("sync")
                            def _(e, i=i, t=t, it=it, j=j):
                                p.wv(e, "ve", ve_done[it])
                                e.dma_start(out=h_hbm[i * 128:(i + 1) * 128, t * 512:(t + 1) * 512],
                                            in_=h_sb[j][:]).then_inc(sems["st"], 16)
                                p.inc("st", 16)
                                st_done[it] = p.val("st")

                    p.barrier()

            MPASS = [(0, 7, True), (7, 14, False), (14, 16, False)]
            with (
                nc.sbuf_tensor("wd_sb", [128, 64 * 7 * 128], BF16) as wd_sb,
                nc.sbuf_tensor("hh_sb", [128, 32 * 512], BF16) as hh_sb,
                nc.sbuf_tensor("nhsum", [1, TPC], BF16) as nhsum,
                nc.sbuf_tensor("xo_sb", [128, 512], F32) as xo_sb,
                nc.sbuf_tensor("xn_sb", [128, 512], F32) as xn_sb,
                nc.psum_tensor("pd", [128, 512 * 7], F32) as pd,
                nc.psum_tensor("phs", [1, 512], F32) as phs,
            ):
                x_src = xsum if with_addend else x

                for (m0, m1, do_cs) in MPASS:
                    nm = m1 - m0
                    @p.on("gpsimd")
                    def _(e, m0=m0, m1=m1, nm=nm):
                        p.w(e, "te", "st")
                        e.dma_start(out=wd_sb[:, 0:64 * nm * 128].rearrange("p (k m) -> p k m", k=64),
                                    in_=wdT[:, m0 * 128:m1 * 128].rearrange(
                                        "(k p) m -> p k m", p=128)).then_inc(sems["gd"], 16)
                        p.inc("gd", 16)

                    for t in range(NT):
                        for ih in range(2):
                            @p.on("sync")
                            def _(e, t=t, ih=ih):
                                p.w(e, "te")
                                if t == 0 and ih == 0:
                                    p.w(e, "st")
                                e.dma_start(out=hh_sb[:].rearrange("p (i n) -> p i n", n=512),
                                            in_=h_hbm[ih * 4096:(ih + 1) * 4096,
                                                      t * 512:(t + 1) * 512].rearrange(
                                                "(i p) n -> p i n", p=128)).then_inc(sems["ld"], 16)
                                p.inc("ld", 16)

                            @p.on("tensor")
                            def _(e, m0=m0, nm=nm, t=t, ih=ih, do_cs=do_cs):
                                if t == 0 and ih == 0:
                                    p.w(e, "gd")
                                p.w(e, "ld")
                                if ih == 0:
                                    p.w(e, "ve")
                                for i in range(32):
                                    iw = ih * 32 + i
                                    for mc in range(nm):
                                        mm = e.matmul(pd[:, mc * 512:(mc + 1) * 512],
                                                      wd_sb[:, (iw * nm + mc) * 128:(iw * nm + mc + 1) * 128],
                                                      hh_sb[:, i * 512:(i + 1) * 512],
                                                      start=(iw == 0), stop=False)
                                    if do_cs:
                                        mm = e.matmul(phs[:], ones_col[:], hh_sb[:, i * 512:(i + 1) * 512],
                                                      start=(iw == 0), stop=(iw == 63))
                                mm.then_inc(sems["te"], 1)
                                p.inc("te", 1)

                        if do_cs:
                            @p.on("vector")
                            def _(e, t=t):
                                p.w(e, "te")
                                e.tensor_scalar_mul(nhsum[:, t * 512:(t + 1) * 512], phs[:], -128.0).then_inc(sems["ve"], 1)
                                p.inc("ve", 1)

                        for mc in range(nm):
                            m = m0 + mc
                            @p.on("tensor")
                            def _(e, mc=mc, t=t, do_cs=do_cs):
                                if do_cs:
                                    p.w(e, "ve")
                                e.matmul(pd[:, mc * 512:(mc + 1) * 512], ones1b[:],
                                         nhsum[:, t * 512:(t + 1) * 512],
                                         start=False, stop=True).then_inc(sems["te"], 1)
                                p.inc("te", 1)

                            @p.on("sync")
                            def _(e, m=m, t=t):
                                p.w(e, "ve")
                                e.dma_start(out=xo_sb[:], in_=x_src[m * 128:(m + 1) * 128,
                                                                    t * 512:(t + 1) * 512]).then_inc(sems["ld"], 16)
                                p.inc("ld", 16)

                            @p.on("vector")
                            def _(e, m=m, mc=mc, t=t):
                                p.w(e, "te", "ld", "st")
                                e.scalar_tensor_tensor(xn_sb[:], pd[:, mc * 512:(mc + 1) * 512],
                                                       sdt2[:, m:m + 1], xo_sb[:],
                                                       ALU.mult, ALU.add).then_inc(sems["ve"], 1)
                                p.inc("ve", 1)

                            @p.on("scalar")
                            def _(e, m=m, t=t):
                                p.w(e, "ve")
                                e.dma_start(out=xout[m * 128:(m + 1) * 128, t * 512:(t + 1) * 512],
                                            in_=xn_sb[:]).then_inc(sems["st"], 16)
                                p.inc("st", 16)

                @p.on("gpsimd")
                def _(e):
                    p.w(e, "st")

    return nc


def emit_ternarize(p, wT_hbm, wq_hbm, nchunks, width, s_out, sc, tag):
    S = p.sems
    nc = p.nc
    wbuf, qa, qb = sc["wbuf"], sc["qa"], sc["qb"]
    abss, onesf = sc["abss"], sc["onesf"]
    sps, bps = sc["sps"], sc["bps"]
    thr_bc, nthr_bc = sc["thr_bc"], sc["nthr_bc"]
    tsc = sc["tsc"]

    for c in range(nchunks):
        b = c % 2
        @p.on("sync")
        def _(e, c=c, b=b):
            if c >= 2:
                p.w(e, "ve")
            e.dma_start(out=wbuf[b][:, 0:width], in_=wT_hbm[c * 128:(c + 1) * 128, :]).then_inc(S["ld"], 16)
            p.inc("ld", 16)

        @p.on("vector")
        def _(e, c=c, b=b):
            p.w(e, "ld")
            e.tensor_reduce(abss[:, c:c + 1], wbuf[b][:, 0:width], mybir.AxisListType.X, ALU.add,
                            apply_absolute_value=True).then_inc(S["ve"], 1)
            p.inc("ve", 1)

    @p.on("tensor")
    def _(e):
        p.w(e, "ve")
        e.matmul(sps[:, 0:nchunks], onesf[:], abss[:, 0:nchunks],
                 start=True, stop=True).then_inc(S["te"], 1)
        p.inc("te", 1)

    @p.on("vector")
    def _(e, nchunks=nchunks, width=width):
        p.w(e, "te")
        e.tensor_reduce(tsc[:, 0:1], sps[:, 0:nchunks], mybir.AxisListType.X, ALU.add).then_inc(S["ve"], 1)
        p.inc("ve", 1)
        p.w(e, "ve")
        e.tensor_scalar_mul(tsc[:, 1:2], tsc[:, 0:1], 1.0 / (nchunks * 128 * width)).then_inc(S["ve"], 1)
        p.inc("ve", 1)
        p.w(e, "ve")
        e.tensor_scalar_max(s_out[:], tsc[:, 1:2], 1e-5).then_inc(S["ve"], 1)
        p.inc("ve", 1)
        p.w(e, "ve")
        e.tensor_scalar_mul(tsc[:, 2:3], s_out[:], 0.7).then_inc(S["ve"], 1)
        p.inc("ve", 1)
        p.w(e, "ve")
        e.tensor_scalar_mul(tsc[:, 3:4], s_out[:], -0.7).then_inc(S["ve"], 1)
        p.inc("ve", 1)

    for src_col, dst in ((2, thr_bc), (3, nthr_bc)):
        @p.on("tensor")
        def _(e, src_col=src_col):
            p.w(e, "ve")
            e.matmul(bps[:, 0:1], sc["ones_row1"][:], tsc[:, src_col:src_col + 1],
                     start=True, stop=True).then_inc(S["te"], 1)
            p.inc("te", 1)

        @p.on("vector")
        def _(e, dst=dst):
            p.w(e, "te")
            e.tensor_copy(dst[:], bps[:, 0:1]).then_inc(S["ve"], 1)
            p.inc("ve", 1)

    for c in range(nchunks):
        b = c % 2
        @p.on("sync")
        def _(e, c=c, b=b):
            p.w(e, "ve")
            if c >= 2:
                p.w(e, "st")
            e.dma_start(out=wbuf[b][:, 0:width], in_=wT_hbm[c * 128:(c + 1) * 128, :]).then_inc(S["ld"], 16)
            p.inc("ld", 16)

        @p.on("vector")
        def _(e, c=c, b=b):
            p.w(e, "ld", "st")
            e.tensor_single_scalar(qa[:, 0:width], wbuf[b][:, 0:width], thr_bc[:], ALU.is_gt).then_inc(S["ve"], 1)
            p.inc("ve", 1)
            e.tensor_single_scalar(qb[:, 0:width], wbuf[b][:, 0:width], nthr_bc[:], ALU.is_lt).then_inc(S["ve"], 1)
            p.inc("ve", 1)
            p.w(e, "ve")
            e.tensor_sub(qa[:, 0:width], qa[:, 0:width], qb[:, 0:width]).then_inc(S["ve"], 1)
            p.inc("ve", 1)

        @p.on("scalar")
        def _(e, c=c):
            p.w(e, "ve")
            e.dma_start(out=wq_hbm[c * 128:(c + 1) * 128, :], in_=qa[:, 0:width]).then_inc(S["st"], 16)
            p.inc("st", 16)

    p.barrier()


def build_l1ffn(pre_wout):
    nc = bass.Bass("TRN2", target_bir_lowering=False)
    x = nc.dram_tensor("x", [D, TPC], F32, kind="ExternalInput").ap()
    wfT = nc.dram_tensor("wfT", [D, 5120], F32, kind="ExternalInput").ap()
    wpT = nc.dram_tensor("wpT", [5120, D], F32, kind="ExternalInput").ap()
    xout = nc.dram_tensor("xout", [D, TPC], F32, kind="ExternalOutput").ap()
    if pre_wout:
        ygT = nc.dram_tensor("ygT", [4096, TPC], BF16, kind="ExternalInput").ap()
        woT = nc.dram_tensor("woT", [4096, D], F32, kind="ExternalInput").ap()
        xmid = nc.dram_tensor("xmid", [D, TPC], F32, kind="Internal").ap()
    wq1 = nc.dram_tensor("wq1", [D, 5120], BF16, kind="Internal").ap()
    wq2 = nc.dram_tensor("wq2", [5120, D], BF16, kind="Internal").ap()
    hr_hbm = nc.dram_tensor("hrscratch", [5120, TPC], BF16, kind="Internal").ap()

    with contextlib.ExitStack() as outer:
        sems = make_sems(nc, outer)
        sbo = lambda name, shape, dt: outer.enter_context(nc.sbuf_tensor(name, shape, dt))
        ones1b = sbo("ones1b", [1, 128], BF16)
        ones_col = sbo("ones_col", [128, 1], BF16)
        ones_row1 = sbo("ones_row1", [1, 128], F32)
        s1 = sbo("s1", [1, 1], F32)
        s2 = sbo("s2", [1, 1], F32)
        scf_bc = sbo("scf_bc", [128, 1], F32)

        with nc.Block() as block:
            p = Prog(nc, block, sems)

            @p.on("vector")
            def _(e):
                e.memset(ones_col[:], 1.0)
                e.memset(ones_row1[:], 1.0)
                e.memset(ones1b[:], 1.0).then_inc(sems["ve"], 1)
                p.inc("ve", 1)

            p.barrier()

            if pre_wout:
                with (
                    nc.sbuf_tensor("wo_sb", [128, 32 * 1024], BF16) as wo_sb,
                    nc.sbuf_tensor("ygt", [128, 32 * 512], BF16) as ygt,
                    nc.sbuf_tensor("xo_sb", [128, 512], F32) as xo_sb,
                    nc.sbuf_tensor("xn_sb", [128, 512], F32) as xn_sb,
                    nc.psum_tensor("pw", [128, 512 * 8], F32) as pw,
                ):
                    for half in range(2):
                        @p.on("gpsimd")
                        def _(e, half=half):
                            p.w(e, "te", "st")
                            e.dma_start(out=wo_sb[:].rearrange("p (k m) -> p k m", k=32),
                                        in_=woT[:, half * 1024:(half + 1) * 1024].rearrange(
                                            "(k p) m -> p k m", p=128)).then_inc(sems["gd"], 16)
                            p.inc("gd", 16)

                        for t in range(NT):
                            @p.on("sync")
                            def _(e, t=t):
                                p.w(e, "te")
                                e.dma_start(out=ygt[:].rearrange("p (i n) -> p i n", n=512),
                                            in_=ygT[:, t * 512:(t + 1) * 512].rearrange(
                                                "(i p) n -> p i n", p=128)).then_inc(sems["ld"], 16)
                                p.inc("ld", 16)

                            @p.on("tensor")
                            def _(e, half=half, t=t):
                                if t == 0:
                                    p.w(e, "gd")
                                p.w(e, "ld", "ve")
                                for mc in range(8):
                                    for i in range(32):
                                        mm = e.matmul(pw[:, mc * 512:(mc + 1) * 512],
                                                      wo_sb[:, (i * 8 + mc) * 128:(i * 8 + mc + 1) * 128],
                                                      ygt[:, i * 512:(i + 1) * 512],
                                                      start=(i == 0), stop=(i == 31))
                                mm.then_inc(sems["te"], 1)
                                p.inc("te", 1)

                            for mc in range(8):
                                m = half * 8 + mc
                                @p.on("sync")
                                def _(e, m=m, t=t):
                                    p.w(e, "ve")
                                    e.dma_start(out=xo_sb[:], in_=x[m * 128:(m + 1) * 128,
                                                                    t * 512:(t + 1) * 512]).then_inc(sems["ld"], 16)
                                    p.inc("ld", 16)

                                @p.on("vector")
                                def _(e, m=m, mc=mc, t=t):
                                    p.w(e, "te", "ld", "st")
                                    e.scalar_tensor_tensor(xn_sb[:], pw[:, mc * 512:(mc + 1) * 512],
                                                           1.0, xo_sb[:],
                                                           ALU.mult, ALU.add).then_inc(sems["ve"], 1)
                                    p.inc("ve", 1)

                                @p.on("scalar")
                                def _(e, m=m, t=t):
                                    p.w(e, "ve")
                                    e.dma_start(out=xmid[m * 128:(m + 1) * 128, t * 512:(t + 1) * 512],
                                                in_=xn_sb[:]).then_inc(sems["st"], 16)
                                    p.inc("st", 16)

                p.barrier()
                x_src = xmid
            else:
                x_src = x

            with contextlib.ExitStack() as tsc_scope:
                sbt = lambda name, shape, dt: tsc_scope.enter_context(nc.sbuf_tensor(name, shape, dt))
                sc = {
                    "wbuf": [sbt("twb0", [128, 5120], F32), sbt("twb1", [128, 5120], F32)],
                    "qa": sbt("tqa", [128, 5120], BF16),
                    "qb": sbt("tqb", [128, 5120], BF16),
                    "abss": sbt("tabss", [128, 40], F32),
                    "onesf": sbt("tonesf", [128, 1], F32),
                    "tsc": sbt("ttsc", [1, 4], F32),
                    "thr_bc": sbt("tthr", [128, 1], F32),
                    "nthr_bc": sbt("tnthr", [128, 1], F32),
                    "ones_row1": ones_row1,
                    "sps": tsc_scope.enter_context(nc.psum_tensor("tsps", [1, 512], F32)),
                    "bps": tsc_scope.enter_context(nc.psum_tensor("tbps", [128, 512], F32)),
                }

                @p.on("vector")
                def _(e):
                    e.memset(sc["onesf"][:], 1.0).then_inc(sems["ve"], 1)
                    p.inc("ve", 1)

                emit_ternarize(p, wfT, wq1, DC, 5120, s1, sc, "f")
                emit_ternarize(p, wpT, wq2, 40, D, s2, sc, "p")

                @p.on("vector")
                def _(e):
                    p.w(e, "ve")
                    e.tensor_mul(sc["tsc"][:, 0:1], s1[:], s1[:]).then_inc(sems["ve"], 1)
                    p.inc("ve", 1)
                    p.w(e, "ve")
                    e.tensor_mul(sc["tsc"][:, 1:2], sc["tsc"][:, 0:1], s2[:]).then_inc(sems["ve"], 1)
                    p.inc("ve", 1)
                    p.w(e, "ve")
                    e.tensor_scalar_mul(sc["tsc"][:, 2:3], sc["tsc"][:, 1:2], 0.5).then_inc(sems["ve"], 1)
                    p.inc("ve", 1)

                @p.on("tensor")
                def _(e):
                    p.w(e, "ve")
                    e.matmul(sc["bps"][:, 0:1], ones_row1[:], sc["tsc"][:, 2:3],
                             start=True, stop=True).then_inc(sems["te"], 1)
                    p.inc("te", 1)

                @p.on("vector")
                def _(e):
                    p.w(e, "te")
                    e.tensor_copy(scf_bc[:], sc["bps"][:, 0:1]).then_inc(sems["ve"], 1)
                    p.inc("ve", 1)

                p.barrier()

            with contextlib.ExitStack() as ph1:
                sb1 = lambda name, shape, dt: ph1.enter_context(nc.sbuf_tensor(name, shape, dt))
                ln_sb = [sb1(f"ln{c}", [128, TPC], BF16) for c in range(DC)]
                emit_ln(p, x_src, ln_sb, ones_col, ones_row1)

                with contextlib.ExitStack() as gsc:
                    sbg = lambda name, shape, dt: gsc.enter_context(nc.sbuf_tensor(name, shape, dt))
                    wf_sb = [sbg("fwb0", [128, D], BF16), sbg("fwb1", [128, D], BF16)]
                    r_sb = [sbg(f"r_sb{j}", [128, 512], BF16) for j in range(2)]
                    h_sb = [sbg(f"h_sb{j}", [128, 512], BF16) for j in range(2)]
                    pf = [gsc.enter_context(nc.psum_tensor(f"pf{j}", [128, 512], F32)) for j in range(2)]
                    ve_done = {}
                    se_done = {}
                    st_done = {}
                    te0 = p.val("te")

                    for i in range(40):
                        b = i % 2
                        @p.on("sync")
                        def _(e, i=i, b=b):
                            if i >= 2:
                                p.wv(e, "te", te0 + (i - 1) * NT)
                            e.dma_start(out=wf_sb[b][:].rearrange("p (k m) -> p k m", m=128),
                                        in_=wq1[:, i * 128:(i + 1) * 128].rearrange(
                                            "(k p) m -> p k m", p=128)).then_inc(sems["ld"], 16)
                            p.inc("ld", 16)

                        for t in range(NT):
                            it = i * NT + t
                            j = it % 2
                            @p.on("tensor")
                            def _(e, i=i, b=b, t=t, it=it, j=j):
                                if t == 0:
                                    p.w(e, "ld")
                                p.wv(e, "ve", ve_done.get(it - 2, 0))
                                for k in range(DC):
                                    mm = e.matmul(pf[j][:], wf_sb[b][:, k * 128:(k + 1) * 128],
                                                  ln_sb[k][:, t * 512:(t + 1) * 512],
                                                  start=(k == 0), stop=(k == DC - 1))
                                mm.then_inc(sems["te"], 1)
                                p.inc("te", 1)

                            @p.on("vector")
                            def _(e, i=i, t=t, it=it, j=j):
                                p.w(e, "te")
                                p.wv(e, "se", se_done.get(it - 2, 0))
                                e.tensor_scalar_max(r_sb[j][:], pf[j][:], 0.0).then_inc(sems["ve"], 1)
                                p.inc("ve", 1)
                                ve_done[it] = p.val("ve")

                            @p.on("scalar")
                            def _(e, i=i, t=t, it=it, j=j):
                                p.wv(e, "ve", ve_done[it])
                                p.wv(e, "st", st_done.get(it - 2, 0))
                                e.activation(h_sb[j][:], r_sb[j][:], AF.Square).then_inc(sems["se"], 1)
                                p.inc("se", 1)
                                se_done[it] = p.val("se")

                            @p.on("gpsimd")
                            def _(e, i=i, t=t, it=it, j=j):
                                p.wv(e, "se", se_done[it])
                                e.dma_start(out=hr_hbm[i * 128:(i + 1) * 128, t * 512:(t + 1) * 512],
                                            in_=h_sb[j][:]).then_inc(sems["st"], 16)
                                p.inc("st", 16)
                                st_done[it] = p.val("st")

                    p.barrier()

            with (
                nc.sbuf_tensor("wp_sb", [128, 40 * 1024], BF16) as wp_sb,
                nc.sbuf_tensor("hh_sbp0", [128, 20 * 512], BF16) as hh_sbp0,
                nc.sbuf_tensor("hh_sbp1", [128, 20 * 512], BF16) as hh_sbp1,
                nc.sbuf_tensor("xo2_sb", [128, 512], F32) as xo_sb,
                nc.sbuf_tensor("xn2_sb", [128, 512], F32) as xn_sb,
                nc.psum_tensor("pp8", [128, 512 * 8], F32) as pp8,
            ):
                hh_sb = [hh_sbp0, hh_sbp1]
                hh_cons = {}
                for half in range(2):
                    @p.on("gpsimd")
                    def _(e, half=half):
                        p.w(e, "te", "st")
                        e.dma_start(out=wp_sb[:].rearrange("p (k m) -> p k m", k=40),
                                    in_=wq2[:, half * 1024:(half + 1) * 1024].rearrange(
                                        "(k p) m -> p k m", p=128)).then_inc(sems["gd"], 16)
                        p.inc("gd", 16)

                    for t in range(NT):
                        for ih in range(2):
                            itd = t * 2 + ih
                            jh = itd % 2
                            @p.on("sync")
                            def _(e, half=half, t=t, ih=ih, itd=itd, jh=jh):
                                if itd < 2:
                                    p.w(e, "te")
                                else:
                                    p.wv(e, "te", hh_cons.get((half, itd - 2), 0))
                                if t == 0 and ih == 0:
                                    p.w(e, "st")
                                e.dma_start(out=hh_sb[jh][:].rearrange("p (i n) -> p i n", n=512),
                                            in_=hr_hbm[ih * 2560:(ih + 1) * 2560,
                                                       t * 512:(t + 1) * 512].rearrange(
                                                "(i p) n -> p i n", p=128)).then_inc(sems["ld"], 16)
                                p.inc("ld", 16)

                            @p.on("tensor")
                            def _(e, half=half, t=t, ih=ih, itd=itd, jh=jh):
                                if t == 0 and ih == 0:
                                    p.w(e, "gd")
                                p.w(e, "ld")
                                if ih == 0:
                                    p.w(e, "ve")
                                for i in range(20):
                                    iw = ih * 20 + i
                                    for mc in range(8):
                                        mm = e.matmul(pp8[:, mc * 512:(mc + 1) * 512],
                                                      wp_sb[:, (iw * 8 + mc) * 128:(iw * 8 + mc + 1) * 128],
                                                      hh_sb[jh][:, i * 512:(i + 1) * 512],
                                                      start=(iw == 0), stop=(iw == 39))
                                mm.then_inc(sems["te"], 1)
                                p.inc("te", 1)
                                hh_cons[(half, itd)] = p.val("te")

                        for mc in range(8):
                            m = half * 8 + mc
                            @p.on("sync")
                            def _(e, m=m, t=t):
                                p.w(e, "ve")
                                e.dma_start(out=xo_sb[:], in_=x_src[m * 128:(m + 1) * 128,
                                                                    t * 512:(t + 1) * 512]).then_inc(sems["ld"], 16)
                                p.inc("ld", 16)

                            @p.on("vector")
                            def _(e, m=m, mc=mc, t=t):
                                p.w(e, "te", "ld", "st")
                                e.scalar_tensor_tensor(xn_sb[:], pp8[:, mc * 512:(mc + 1) * 512],
                                                       scf_bc[:], xo_sb[:],
                                                       ALU.mult, ALU.add).then_inc(sems["ve"], 1)
                                p.inc("ve", 1)

                            @p.on("scalar")
                            def _(e, m=m, t=t):
                                p.w(e, "ve")
                                e.dma_start(out=xout[m * 128:(m + 1) * 128, t * 512:(t + 1) * 512],
                                            in_=xn_sb[:]).then_inc(sems["st"], 16)
                                p.inc("st", 16)

            @p.on("gpsimd")
            def _(e):
                p.w(e, "st")

    return nc


def build_prescan():
    nc = bass.Bass("TRN2", target_bir_lowering=False)
    x = nc.dram_tensor("x", [D, TPC], F32, kind="ExternalInput").ap()
    winT = nc.dram_tensor("winT", [D, 8192], F32, kind="ExternalInput").ap()
    xzout = nc.dram_tensor("xzout", [8192, TPC], BF16, kind="ExternalOutput").ap()

    with contextlib.ExitStack() as outer:
        sems = make_sems(nc, outer)
        sbo = lambda name, shape, dt: outer.enter_context(nc.sbuf_tensor(name, shape, dt))
        ones_col = sbo("ones_col", [128, 1], BF16)
        ones_row1 = sbo("ones_row1", [1, 128], F32)

        with nc.Block() as block:
            p = Prog(nc, block, sems)

            @p.on("vector")
            def _(e):
                e.memset(ones_col[:], 1.0)
                e.memset(ones_row1[:], 1.0).then_inc(sems["ve"], 1)
                p.inc("ve", 1)

            p.barrier()

            with contextlib.ExitStack() as ph1:
                sb1 = lambda name, shape, dt: ph1.enter_context(nc.sbuf_tensor(name, shape, dt))
                ln_sb = [sb1(f"ln{c}", [128, TPC], BF16) for c in range(DC)]
                emit_ln(p, x, ln_sb, ones_col, ones_row1)

                with contextlib.ExitStack() as gsc:
                    sbg = lambda name, shape, dt: gsc.enter_context(nc.sbuf_tensor(name, shape, dt))
                    wi_sb = [sbg("iwb0", [128, D], BF16), sbg("iwb1", [128, D], BF16)]
                    o_sb = [sbg(f"o_sb{j}", [128, 512], BF16) for j in range(2)]
                    pf = [gsc.enter_context(nc.psum_tensor(f"pf{j}", [128, 512], F32)) for j in range(2)]
                    ev_done = {}
                    st_done = {}
                    te0 = p.val("te")

                    for i in range(64):
                        b = i % 2
                        @p.on("gpsimd")
                        def _(e, i=i, b=b):
                            if i >= 2:
                                p.wv(e, "te", te0 + (i - 1) * NT)
                            e.dma_start(out=wi_sb[b][:].rearrange("p (k m) -> p k m", m=128),
                                        in_=winT[:, i * 128:(i + 1) * 128].rearrange(
                                            "(k p) m -> p k m", p=128)).then_inc(sems["gd"], 16)
                            p.inc("gd", 16)

                        for t in range(NT):
                            it = i * NT + t
                            j = it % 2
                            @p.on("tensor")
                            def _(e, i=i, b=b, t=t, it=it, j=j):
                                if t == 0:
                                    p.w(e, "gd")
                                prev = ev_done.get(it - 2)
                                if prev is not None:
                                    p.wv(e, prev[0], prev[1])
                                for k in range(DC):
                                    mm = e.matmul(pf[j][:], wi_sb[b][:, k * 128:(k + 1) * 128],
                                                  ln_sb[k][:, t * 512:(t + 1) * 512],
                                                  start=(k == 0), stop=(k == DC - 1))
                                mm.then_inc(sems["te"], 1)
                                p.inc("te", 1)

                            if i < 32:
                                @p.on("vector")
                                def _(e, i=i, t=t, it=it, j=j):
                                    p.w(e, "te")
                                    p.wv(e, "st", st_done.get(it - 2, 0))
                                    e.tensor_copy(o_sb[j][:], pf[j][:]).then_inc(sems["ve"], 1)
                                    p.inc("ve", 1)
                                    ev_done[it] = ("ve", p.val("ve"))
                            else:
                                @p.on("scalar")
                                def _(e, i=i, t=t, it=it, j=j):
                                    p.w(e, "te")
                                    p.wv(e, "st", st_done.get(it - 2, 0))
                                    e.activation(o_sb[j][:], pf[j][:], AF.Silu).then_inc(sems["se"], 1)
                                    p.inc("se", 1)
                                    ev_done[it] = ("se", p.val("se"))

                            @p.on("sync")
                            def _(e, i=i, t=t, it=it, j=j):
                                ev = ev_done[it]
                                p.wv(e, ev[0], ev[1])
                                e.dma_start(out=xzout[i * 128:(i + 1) * 128, t * 512:(t + 1) * 512],
                                            in_=o_sb[j][:]).then_inc(sems["st"], 16)
                                p.inc("st", 16)
                                st_done[it] = p.val("st")

            @p.on("gpsimd")
            def _(e):
                p.w(e, "st")

    return nc


LSEQ = 16384
def build_mambapre():
    nc = bass.Bass("TRN2", target_bir_lowering=False)
    xzx = nc.dram_tensor("xzx", [512, 2 * LSEQ], BF16, kind="ExternalInput").ap()
    convw = nc.dram_tensor("convw", [128, 16], F32, kind="ExternalInput").ap()
    convb = nc.dram_tensor("convb", [128, 4], F32, kind="ExternalInput").ap()
    wxT = nc.dram_tensor("wxT", [512, 160], F32, kind="ExternalInput").ap()
    xcout = nc.dram_tensor("xcout", [512, 2 * LSEQ], BF16, kind="ExternalOutput").ap()
    pssm = nc.dram_tensor("pssm", [160, 2 * LSEQ], F32, kind="ExternalOutput").ap()

    with contextlib.ExitStack() as outer:
        sems = make_sems(nc, outer)
        sbo = lambda name, shape, dt: outer.enter_context(nc.sbuf_tensor(name, shape, dt))
        cw = sbo("cw", [128, 16], F32)
        cb = sbo("cb", [128, 4], F32)
        wx_sb = sbo("wx_sb", [128, 4 * 160], BF16)
        xzt = sbo("xzt", [128, LSEQ + 3], BF16)
        acc = sbo("acc", [128, LSEQ], F32)
        tp = sbo("tp", [128, LSEQ], F32)
        xct = sbo("xct", [128, LSEQ], BF16)
        xcl = sbo("xcl", [128, 4 * 512], BF16)
        ev0 = sbo("ev0", [128, 512], F32)
        ev1 = sbo("ev1", [32, 512], F32)
        p0 = outer.enter_context(nc.psum_tensor("p0", [128, 512], F32))
        p1 = outer.enter_context(nc.psum_tensor("p1", [32, 512], F32))

        with nc.Block() as block:
            p = Prog(nc, block, sems)

            @p.on("sync")
            def _(e):
                e.dma_start(out=cw[:], in_=convw[:]).then_inc(sems["ld"], 16)
                e.dma_start(out=cb[:], in_=convb[:]).then_inc(sems["ld"], 16)
                p.inc("ld", 32)

            @p.on("gpsimd")
            def _(e):
                e.dma_start(out=wx_sb[:].rearrange("p (k m) -> p k m", m=160),
                            in_=wxT.rearrange("(k p) m -> p k m", p=128)).then_inc(sems["gd"], 16)
                p.inc("gd", 16)

            @p.on("vector")
            def _(e):
                e.memset(xzt[:, 0:3], 0.0).then_inc(sems["ve"], 1)
                p.inc("ve", 1)

            p.barrier()

            for db in range(4):
                for bb in range(2):
                    @p.on("sync")
                    def _(e, db=db, bb=bb):
                        p.w(e, "ve", "st")
                        e.dma_start(out=xzt[:, 3:LSEQ + 3],
                                    in_=xzx[db * 128:(db + 1) * 128,
                                            bb * LSEQ:(bb + 1) * LSEQ]).then_inc(sems["ld"], 16)
                        p.inc("ld", 16)

                    @p.on("vector")
                    def _(e, db=db, bb=bb):
                        p.w(e, "ld", "se")
                        e.tensor_single_scalar(acc[:], xzt[:, 0:LSEQ], cw[:, db * 4:db * 4 + 1],
                                               ALU.mult).then_inc(sems["ve"], 1)
                        p.inc("ve", 1)
                        for k in range(1, 4):
                            p.w(e, "ve")
                            e.scalar_tensor_tensor(acc[:], xzt[:, k:LSEQ + k], cw[:, db * 4 + k:db * 4 + k + 1],
                                                   acc[:], ALU.mult, ALU.add).then_inc(sems["ve"], 1)
                            p.inc("ve", 1)

                    @p.on("scalar")
                    def _(e, db=db, bb=bb):
                        p.w(e, "ve", "st")
                        e.activation(xct[:], acc[:], AF.Silu, bias=cb[:, db:db + 1]).then_inc(sems["se"], 1)
                        p.inc("se", 1)

                    @p.on("sync")
                    def _(e, db=db, bb=bb):
                        p.w(e, "se")
                        e.dma_start(out=xcout[db * 128:(db + 1) * 128, bb * LSEQ:(bb + 1) * LSEQ],
                                    in_=xct[:]).then_inc(sems["st"], 16)
                        p.inc("st", 16)

            p.barrier()

            for tt in range(2 * LSEQ // 512):
                @p.on("sync")
                def _(e, tt=tt):
                    p.w(e, "te")
                    e.dma_start(out=xcl[:].rearrange("p (k n) -> p k n", n=512),
                                in_=xcout[:, tt * 512:(tt + 1) * 512].rearrange(
                                    "(k p) n -> p k n", p=128)).then_inc(sems["ld"], 16)
                    p.inc("ld", 16)

                @p.on("tensor")
                def _(e, tt=tt):
                    if tt == 0:
                        p.w(e, "gd")
                    p.w(e, "ld", "ve")
                    for k in range(4):
                        e.matmul(p0[:], wx_sb[:, (k * 160):(k * 160 + 128)],
                                 xcl[:, k * 512:(k + 1) * 512], start=(k == 0), stop=(k == 3))
                        mm = e.matmul(p1[:], wx_sb[:, (k * 160 + 128):(k * 160 + 160)],
                                      xcl[:, k * 512:(k + 1) * 512], start=(k == 0), stop=(k == 3))
                    mm.then_inc(sems["te"], 1)
                    p.inc("te", 1)

                @p.on("vector")
                def _(e, tt=tt):
                    p.w(e, "te", "st")
                    e.tensor_copy(ev0[:], p0[:]).then_inc(sems["ve"], 1)
                    p.inc("ve", 1)
                    e.tensor_copy(ev1[:], p1[:]).then_inc(sems["ve"], 1)
                    p.inc("ve", 1)

                @p.on("scalar")
                def _(e, tt=tt):
                    p.w(e, "ve")
                    e.dma_start(out=pssm[0:128, tt * 512:(tt + 1) * 512], in_=ev0[:]).then_inc(sems["st"], 16)
                    e.dma_start(out=pssm[128:160, tt * 512:(tt + 1) * 512], in_=ev1[:]).then_inc(sems["st"], 16)
                    p.inc("st", 32)

            @p.on("gpsimd")
            def _(e):
                p.w(e, "st")

    return nc


SEG = 2048
def build_mambascan():
    nc = bass.Bass("TRN2", target_bir_lowering=False)
    xc = nc.dram_tensor("xc", [512, 2 * LSEQ], BF16, kind="ExternalInput").ap()
    sz = nc.dram_tensor("sz", [512, 2 * LSEQ], BF16, kind="ExternalInput").ap()
    pssm = nc.dram_tensor("pssm", [8 * 160, 2 * LSEQ], F32, kind="ExternalInput").ap()
    wdtT = nc.dram_tensor("wdtT", [128, 512], F32, kind="ExternalInput").ap()
    dtb = nc.dram_tensor("dtb", [128, 4], F32, kind="ExternalInput").ap()
    Asl = nc.dram_tensor("Asl", [128, 64], F32, kind="ExternalInput").ap()
    Dsl = nc.dram_tensor("Dsl", [128, 4], F32, kind="ExternalInput").ap()
    ygout = nc.dram_tensor("ygout", [512, 2 * LSEQ], BF16, kind="ExternalOutput").ap()

    NSEG = LSEQ // SEG
    NTT = SEG // 512

    with contextlib.ExitStack() as outer:
        sems = make_sems(nc, outer)
        sbo = lambda name, shape, dt: outer.enter_context(nc.sbuf_tensor(name, shape, dt))
        ones_row1 = sbo("ones_row1", [1, 128], F32)
        wdt_sb = sbo("wdt_sb", [128, 512], BF16)
        dtb_sb = sbo("dtb_sb", [128, 4], F32)
        A_sb = sbo("A_sb", [128, 64], F32)
        D_sb = sbo("D_sb", [128, 4], F32)
        indB = sbo("indB", [32, 16 * 128], BF16)
        indC = sbo("indC", [32, 16 * 128], BF16)
        carry = sbo("carry", [128, 64], F32)
        m1c = sbo("m1c", [128, 1], F32)
        p1c = sbo("p1c", [128, 1], F32)
        dtr = sbo("dtr", [128, SEG], F32)
        bcs = sbo("bcs", [32, SEG], F32)
        dtr_bf = sbo("dtr_bf", [128, SEG], BF16)
        bcs_bf = sbo("bcs_bf", [32, SEG], BF16)
        delta = sbo("delta", [128, 4 * SEG], F32)
        sfa = sbo("sfa", [128, SEG], F32)
        du = sbo("du", [128, 4 * SEG], BF16)
        xcl = sbo("xcl", [128, 4 * SEG], BF16)
        szl = sbo("szl", [128, 4 * SEG], BF16)
        yacc = sbo("yacc", [128, 4 * SEG], F32)
        Bbc = sbo("Bbc", [128, SEG], BF16)
        Cbc = sbo("Cbc", [128, SEG], BF16)
        dA2 = [sbo(f"dA2{j}", [128, SEG], F32) for j in range(2)]
        bn = sbo("bn", [128, SEG], BF16)
        ss = sbo("ss", [128, SEG], BF16)
        prod = sbo("prod", [128, SEG], BF16)
        yg = sbo("yg", [128, SEG], BF16)
        pA = outer.enter_context(nc.psum_tensor("pA", [128, 512], F32))
        pB = outer.enter_context(nc.psum_tensor("pB", [32, 512], F32))
        pC = [outer.enter_context(nc.psum_tensor(f"pC{j}", [128, 512], F32)) for j in range(2)]

        with nc.Block() as block:
            p = Prog(nc, block, sems)

            @p.on("sync")
            def _(e):
                for dst, src in ((dtb_sb, dtb), (A_sb, Asl), (D_sb, Dsl)):
                    e.dma_start(out=dst[:], in_=src[:]).then_inc(sems["ld"], 16)
                    p.inc("ld", 16)

            @p.on("gpsimd")
            def _(e):
                e.dma_start(out=wdt_sb[:], in_=wdtT[:]).then_inc(sems["gd"], 16)
                p.inc("gd", 16)
                e.memset(indB[:], 0.0)
                e.affine_select(out=indB[:].rearrange("q (n m) -> q n m", m=128),
                                in_=indB[:].rearrange("q (n m) -> q n m", m=128),
                                compare_op=ALU.not_equal, fill=1.0, base=0,
                                pattern=[[-1, 16], [0, 128]], channel_multiplier=1).then_inc(sems["gd"], 1)
                p.inc("gd", 1)
                e.memset(indC[:], 0.0)
                e.affine_select(out=indC[:].rearrange("q (n m) -> q n m", m=128),
                                in_=indC[:].rearrange("q (n m) -> q n m", m=128),
                                compare_op=ALU.not_equal, fill=1.0, base=-16,
                                pattern=[[-1, 16], [0, 128]], channel_multiplier=1).then_inc(sems["gd"], 1)
                p.inc("gd", 1)

            @p.on("vector")
            def _(e):
                e.memset(carry[:], 0.0)
                e.memset(m1c[:], -1.0)
                e.memset(p1c[:], 1.0).then_inc(sems["ve"], 1)
                p.inc("ve", 1)

            p.barrier()

            pc_done, da_done, scan_done = {}, {}, {}
            for bb in range(2):
                for sg in range(NSEG):
                    t0 = bb * LSEQ + sg * SEG
                    for j in range(8):
                        @p.on("gpsimd")
                        def _(e, j=j, t0=t0):
                            if j == 0:
                                p.w(e, "ve", "te")
                            else:
                                p.w(e, "gd")
                            kw = {} if j == 0 else {"accum_op": ALU.add}
                            e.dma_start(out=dtr[:], in_=pssm[j * 160:j * 160 + 128,
                                                             t0:t0 + SEG], **kw).then_inc(sems["gd"], 16)
                            p.inc("gd", 16)
                            p.w(e, "gd")
                            e.dma_start(out=bcs[:], in_=pssm[j * 160 + 128:j * 160 + 160,
                                                             t0:t0 + SEG], **kw).then_inc(sems["gd"], 16)
                            p.inc("gd", 16)

                    @p.on("vector")
                    def _(e):
                        p.w(e, "ve", "gd")
                        e.tensor_copy(dtr_bf[:], dtr[:]).then_inc(sems["ve"], 1)
                        p.inc("ve", 1)
                        e.tensor_copy(bcs_bf[:], bcs[:]).then_inc(sems["ve"], 1)
                        p.inc("ve", 1)

                    @p.on("sync")
                    def _(e, t0=t0):
                        p.w(e, "ve", "st")
                        e.dma_start(out=xcl[:].rearrange("p (k n) -> p k n", n=SEG),
                                    in_=xc[:, t0:t0 + SEG].rearrange("(k p) n -> p k n", p=128)).then_inc(sems["ld"], 16)
                        e.dma_start(out=szl[:].rearrange("p (k n) -> p k n", n=SEG),
                                    in_=sz[:, t0:t0 + SEG].rearrange("(k p) n -> p k n", p=128)).then_inc(sems["ld"], 16)
                        p.inc("ld", 32)

                    for db in range(4):
                        for tt in range(NTT):
                            @p.on("tensor")
                            def _(e, db=db, tt=tt):
                                if db == 0 and tt == 0:
                                    p.w(e, "gd")
                                p.w(e, "ve")
                                e.matmul(pA[:], wdt_sb[:, db * 128:(db + 1) * 128],
                                         dtr_bf[:, tt * 512:(tt + 1) * 512],
                                         start=True, stop=True).then_inc(sems["te"], 1)
                                p.inc("te", 1)

                            @p.on("vector")
                            def _(e, db=db, tt=tt):
                                p.w(e, "te")
                                e.tensor_single_scalar(delta[:, db * SEG + tt * 512:db * SEG + (tt + 1) * 512],
                                                       pA[:], dtb_sb[:, db:db + 1], ALU.add).then_inc(sems["ve"], 1)
                                p.inc("ve", 1)

                        @p.on("scalar")
                        def _(e, db=db):
                            p.w(e, "ve")
                            e.activation(sfa[:], delta[:, db * SEG:(db + 1) * SEG], AF.Abs).then_inc(sems["se"], 1)
                            p.inc("se", 1)
                            p.w(e, "se")
                            e.activation(sfa[:], sfa[:], AF.Exp, scale=m1c[:]).then_inc(sems["se"], 1)
                            p.inc("se", 1)
                            p.w(e, "se")
                            e.activation(sfa[:], sfa[:], AF.Ln, bias=p1c[:]).then_inc(sems["se"], 1)
                            p.inc("se", 1)

                        @p.on("vector")
                        def _(e, db=db):
                            p.w(e, "se")
                            e.tensor_scalar_max(dA2[0][:], delta[:, db * SEG:(db + 1) * SEG], 0.0).then_inc(sems["ve"], 1)
                            p.inc("ve", 1)
                            p.w(e, "ve")
                            e.tensor_add(delta[:, db * SEG:(db + 1) * SEG], dA2[0][:], sfa[:]).then_inc(sems["ve"], 1)
                            p.inc("ve", 1)
                            p.w(e, "ve", "ld")
                            e.tensor_mul(du[:, db * SEG:(db + 1) * SEG],
                                         delta[:, db * SEG:(db + 1) * SEG],
                                         xcl[:, db * SEG:(db + 1) * SEG]).then_inc(sems["ve"], 1)
                            p.inc("ve", 1)

                    @p.on("vector")
                    def _(e):
                        p.w(e, "ve")
                        e.memset(yacc[:], 0.0).then_inc(sems["ve"], 1)
                        p.inc("ve", 1)

                    for n in range(16):
                        for src_ind, dst_bc in ((indB, Bbc), (indC, Cbc)):
                            for tt in range(NTT):
                                jj = tt % 2
                                @p.on("tensor")
                                def _(e, n=n, tt=tt, jj=jj, src_ind=src_ind, dst_bc=dst_bc):
                                    p.wv(e, "ve", pc_done.get(jj, 0))
                                    e.matmul(pC[jj][:], src_ind[:, n * 128:(n + 1) * 128],
                                             bcs_bf[:, tt * 512:(tt + 1) * 512],
                                             start=True, stop=True).then_inc(sems["te"], 1)
                                    p.inc("te", 1)

                                @p.on("vector")
                                def _(e, n=n, tt=tt, jj=jj, dst_bc=dst_bc):
                                    p.w(e, "te")
                                    e.tensor_copy(dst_bc[:, tt * 512:(tt + 1) * 512], pC[jj][:]).then_inc(sems["ve"], 1)
                                    p.inc("ve", 1)
                                    pc_done[jj] = p.val("ve")

                        for db in range(4):
                            col = n * 4 + db
                            jd = db % 2
                            @p.on("scalar")
                            def _(e, n=n, db=db, jd=jd):
                                prev = scan_done.get((n, db - 2), scan_done.get((n - 1, db + 2)))
                                if prev is None:
                                    p.w(e, "ve")
                                else:
                                    p.wv(e, "ve", prev)
                                e.activation(dA2[jd][:], delta[:, db * SEG:(db + 1) * SEG], AF.Exp,
                                             scale=A_sb[:, db * 16 + n:db * 16 + n + 1]).then_inc(sems["se"], 1)
                                p.inc("se", 1)
                                da_done[(n, db)] = p.val("se")

                            @p.on("vector")
                            def _(e, n=n, db=db, bb=bb, sg=sg, col=col, jd=jd):
                                p.w(e, "ve")
                                e.tensor_mul(bn[:], du[:, db * SEG:(db + 1) * SEG], Bbc[:]).then_inc(sems["ve"], 1)
                                p.inc("ve", 1)
                                p.wv(e, "se", da_done[(n, db)])
                                p.w(e, "ve")
                                if sg == 0:
                                    e.tensor_tensor_scan(ss[:], dA2[jd][:], bn[:], 0.0,
                                                         ALU.mult, ALU.add).then_inc(sems["ve"], 1)
                                else:
                                    e.tensor_tensor_scan(ss[:], dA2[jd][:], bn[:], carry[:, col:col + 1],
                                                         ALU.mult, ALU.add).then_inc(sems["ve"], 1)
                                p.inc("ve", 1)
                                scan_done[(n, db)] = p.val("ve")
                                p.w(e, "ve")
                                e.tensor_copy(carry[:, col:col + 1], ss[:, SEG - 1:SEG]).then_inc(sems["ve"], 1)
                                p.inc("ve", 1)
                                e.tensor_mul(prod[:], ss[:], Cbc[:]).then_inc(sems["ve"], 1)
                                p.inc("ve", 1)
                                p.w(e, "ve")
                                e.tensor_add(yacc[:, db * SEG:(db + 1) * SEG],
                                             yacc[:, db * SEG:(db + 1) * SEG], prod[:]).then_inc(sems["ve"], 1)
                                p.inc("ve", 1)

                    da_done.clear()
                    scan_done.clear()

                    for db in range(4):
                        @p.on("vector")
                        def _(e, db=db):
                            p.w(e, "ve", "st")
                            e.scalar_tensor_tensor(sfa[:], xcl[:, db * SEG:(db + 1) * SEG],
                                                   D_sb[:, db:db + 1], yacc[:, db * SEG:(db + 1) * SEG],
                                                   ALU.mult, ALU.add).then_inc(sems["ve"], 1)
                            p.inc("ve", 1)
                            p.w(e, "ve")
                            e.tensor_mul(yg[:], sfa[:], szl[:, db * SEG:(db + 1) * SEG]).then_inc(sems["ve"], 1)
                            p.inc("ve", 1)

                        @p.on("scalar")
                        def _(e, db=db, t0=t0):
                            p.w(e, "ve")
                            e.dma_start(out=ygout[db * 128:(db + 1) * 128, t0:t0 + SEG],
                                        in_=yg[:]).then_inc(sems["st"], 16)
                            p.inc("st", 16)

            @p.on("gpsimd")
            def _(e):
                p.w(e, "st")

    return nc


QSC = 0.35355339059327373


def build_attn():
    nc = bass.Bass("TRN2", target_bir_lowering=False)
    x = nc.dram_tensor("x", [D, TPC], F32, kind="ExternalInput").ap()
    bdqk = [nc.dram_tensor(f"bdqk{i}", [128, 128], F32, kind="ExternalInput").ap() for i in range(2)]
    bdv = [nc.dram_tensor(f"bdv{i}", [128, 64], F32, kind="ExternalInput").ap() for i in range(2)]
    bdvs = [nc.dram_tensor(f"bdvs{i}", [128, 3 * 64], F32, kind="ExternalInput").ap() for i in range(2)]
    taps = [nc.dram_tensor(f"taps{i}", [1, 3 * 64], F32, kind="ExternalInput").ap() for i in range(2)]
    cbr = [nc.dram_tensor(f"cbr{i}", [1, 1024], F32, kind="ExternalInput").ap() for i in range(2)]
    bdproj = nc.dram_tensor("bdproj", [128, 128], F32, kind="ExternalInput").ap()
    pb_slot = nc.dram_tensor("pb_slot", [128, 1], F32, kind="ExternalInput").ap()
    lnw_slot = nc.dram_tensor("lnw_slot", [128, 1], F32, kind="ExternalInput").ap()
    lnb_slot = nc.dram_tensor("lnb_slot", [128, 1], F32, kind="ExternalInput").ap()
    sel8 = nc.dram_tensor("sel8", [128, 8], F32, kind="ExternalInput").ap()
    exp8 = nc.dram_tensor("exp8", [8, 128], F32, kind="ExternalInput").ap()
    attout = nc.dram_tensor("attout", [D, TPC], BF16, kind="ExternalOutput").ap()
    ln_hbm = nc.dram_tensor("lnscratch", [D, TPC], BF16, kind="Internal").ap()
    lntok = nc.dram_tensor("lntok", [TPC * 128, 16], BF16, kind="Internal").ap()
    b2tok = nc.dram_tensor("b2tok", [TPC * 128, 8], BF16, kind="Internal").ap()

    with contextlib.ExitStack() as outer:
        sems = make_sems(nc, outer)
        sbo = lambda name, shape, dt: outer.enter_context(nc.sbuf_tensor(name, shape, dt))
        ones_col = sbo("ones_col", [128, 1], BF16)
        ones_row1 = sbo("ones_row1", [1, 128], F32)
        ident = sbo("ident", [128, 128], BF16)
        mmA = [sbo(f"mmA{i}", [128, 128], BF16) for i in range(2)]
        mmB = [sbo(f"mmB{i}", [128, 128], BF16) for i in range(2)]
        bdp_sb = sbo("bdp_sb", [128, 128], BF16)
        cbbc = [sbo(f"cbbc{i}", [128, 1024], F32) for i in range(2)]
        pbv = sbo("pbv", [128, 1], F32)
        lnwv = sbo("lnwv", [128, 1], F32)
        lnbv = sbo("lnbv", [128, 1], F32)
        sel8_sb = sbo("sel8_sb", [128, 8], BF16)
        exp8_sb = sbo("exp8_sb", [8, 128], BF16)
        qscv = sbo("qscv", [128, 1], F32)
        epsv = sbo("epsv", [8, 1], F32)
        mhalf = sbo("mhalf", [8, 1], F32)

        with nc.Block() as block:
            p = Prog(nc, block, sems)

            @p.on("vector")
            def _(e):
                e.memset(ones_col[:], 1.0)
                e.memset(ones_row1[:], 1.0)
                e.memset(qscv[:], QSC)
                e.memset(epsv[:], EPS)
                e.memset(mhalf[:], -0.5).then_inc(sems["ve"], 1)
                p.inc("ve", 1)

            @p.on("gpsimd")
            def _(e):
                e.memset(ident[:], 0.0)
                e.affine_select(out=ident[:], in_=ident[:], compare_op=ALU.not_equal,
                                fill=1.0, base=0, pattern=[[-1, 128]],
                                channel_multiplier=1).then_inc(sems["gd"], 1)
                p.inc("gd", 1)
                for i in range(2):
                    e.dma_start(out=mmA[i][:], in_=bdqk[i][:]).then_inc(sems["gd"], 16)
                    e.dma_start(out=mmB[i][:, 0:64], in_=bdv[i][:]).then_inc(sems["gd"], 16)
                    p.inc("gd", 32)
                e.dma_start(out=bdp_sb[:], in_=bdproj[:]).then_inc(sems["gd"], 16)
                e.dma_start(out=sel8_sb[:], in_=sel8[:]).then_inc(sems["gd"], 16)
                e.dma_start(out=exp8_sb[:], in_=exp8[:]).then_inc(sems["gd"], 16)
                p.inc("gd", 48)

            @p.on("sync")
            def _(e):
                for dst, src in ((pbv, pb_slot), (lnwv, lnw_slot), (lnbv, lnb_slot)):
                    e.dma_start(out=dst[:], in_=src[:]).then_inc(sems["ld"], 16)
                    p.inc("ld", 16)

            with contextlib.ExitStack() as cscope:
                sbc = lambda n_, s_, d_: cscope.enter_context(nc.sbuf_tensor(n_, s_, d_))
                tvs = sbc("tvs", [128, 3 * 64], F32)
                trow = sbc("trow", [1, 3 * 64], F32)
                cbrow = sbc("cbrow", [1, 1024], F32)
                tbc = sbc("tbc", [128, 64], F32)
                lwacc = sbc("lwacc", [128, 64], F32)
                pbc = cscope.enter_context(nc.psum_tensor("pbc", [128, 512], F32))

                for i in range(2):
                    @p.on("sync")
                    def _(e, i=i):
                        p.w(e, "te", "ve")
                        e.dma_start(out=tvs[:], in_=bdvs[i][:]).then_inc(sems["ld"], 16)
                        e.dma_start(out=trow[:], in_=taps[i][:]).then_inc(sems["ld"], 16)
                        e.dma_start(out=cbrow[:], in_=cbr[i][:]).then_inc(sems["ld"], 16)
                        p.inc("ld", 48)

                    for dd in range(3):
                        @p.on("tensor")
                        def _(e, i=i, dd=dd):
                            p.w(e, "ld", "ve")
                            e.matmul(pbc[:, 0:64], ones_row1[:], trow[:, dd * 64:(dd + 1) * 64],
                                     start=True, stop=True).then_inc(sems["te"], 1)
                            p.inc("te", 1)

                        @p.on("vector")
                        def _(e, i=i, dd=dd):
                            p.w(e, "te")
                            e.tensor_copy(tbc[:], pbc[:, 0:64]).then_inc(sems["ve"], 1)
                            p.inc("ve", 1)
                            p.w(e, "ve")
                            if dd == 0:
                                e.tensor_mul(lwacc[:], tvs[:, 0:64], tbc[:]).then_inc(sems["ve"], 1)
                                p.inc("ve", 1)
                            else:
                                e.tensor_mul(tbc[:], tvs[:, dd * 64:(dd + 1) * 64], tbc[:]).then_inc(sems["ve"], 1)
                                p.inc("ve", 1)
                                p.w(e, "ve")
                                e.tensor_add(lwacc[:], lwacc[:], tbc[:]).then_inc(sems["ve"], 1)
                                p.inc("ve", 1)

                    @p.on("vector")
                    def _(e, i=i):
                        p.w(e, "ve")
                        e.tensor_copy(mmB[i][:, 64:128], lwacc[:]).then_inc(sems["ve"], 1)
                        p.inc("ve", 1)

                    for t4 in range(2):
                        @p.on("tensor")
                        def _(e, i=i, t4=t4):
                            p.w(e, "ld", "ve")
                            e.matmul(pbc[:], ones_row1[:], cbrow[:, t4 * 512:(t4 + 1) * 512],
                                     start=True, stop=True).then_inc(sems["te"], 1)
                            p.inc("te", 1)

                        @p.on("vector")
                        def _(e, i=i, t4=t4):
                            p.w(e, "te")
                            e.tensor_copy(cbbc[i][:, t4 * 512:(t4 + 1) * 512], pbc[:]).then_inc(sems["ve"], 1)
                            p.inc("ve", 1)

                p.barrier()

            with contextlib.ExitStack() as ph1:
                sb1 = lambda name, shape, dt: ph1.enter_context(nc.sbuf_tensor(name, shape, dt))
                ln_sb = [sb1(f"ln{c}", [128, TPC], BF16) for c in range(DC)]
                emit_ln(p, x, ln_sb, ones_col, ones_row1)

                for c in range(DC):
                    @p.on("scalar")
                    def _(e, c=c):
                        e.dma_start(out=ln_hbm[c * 128:(c + 1) * 128, :], in_=ln_sb[c][:]).then_inc(sems["st"], 16)
                        p.inc("st", 16)

                with (
                    nc.sbuf_tensor("rowt", [128, 2048], BF16) as rowt,
                    nc.psum_tensor("ptr", [128, 512], BF16) as ptr,
                ):
                    for blk in range(32):
                        for cg in range(4):
                            @p.on("tensor")
                            def _(e, blk=blk, cg=cg):
                                p.w(e, "ve")
                                for cc in range(4):
                                    C = cg * 4 + cc
                                    e.transpose(ptr[:, cc * 128:(cc + 1) * 128],
                                                ln_sb[C][:, blk * 128:(blk + 1) * 128],
                                                ident[:]).then_inc(sems["te"], 1)
                                    p.inc("te", 1)

                            @p.on("vector")
                            def _(e, blk=blk, cg=cg):
                                p.w(e, "te")
                                if cg == 0:
                                    p.w(e, "st")
                                e.tensor_copy(rowt[:, cg * 512:(cg + 1) * 512], ptr[:]).then_inc(sems["ve"], 1)
                                p.inc("ve", 1)

                        @p.on("sync")
                        def _(e, blk=blk):
                            p.w(e, "ve")
                            e.dma_start(out=lntok[blk * 16384:(blk + 1) * 16384, :].rearrange(
                                            "(r q) c -> r (q c)", r=128),
                                        in_=rowt[:]).then_inc(sems["st"], 16)
                            p.inc("st", 16)

            p.barrier()

            with contextlib.ExitStack() as cs:
                sbc = lambda n_, s_, d_: cs.enter_context(nc.sbuf_tensor(n_, s_, d_))
                psc = lambda n_, s_, d_: cs.enter_context(nc.psum_tensor(n_, s_, d_))
                fslab = sbc("fslab", [128, 16 * 512], BF16)
                rm2 = sbc("rm2", [128, 2048], BF16)
                SQK = sbc("SQK", [128, 16 * 512], BF16)
                SVL = sbc("SVL", [128, 16 * 512], BF16)
                RM0 = sbc("RM0", [128, 2048], BF16)
                RM1 = sbc("RM1", [128, 2048], BF16)
                sp = sbc("sp", [128, 8192], BF16)
                Ssc = sbc("Ssc", [128, 1024], F32)
                Mx = sbc("Mx", [128, 128], F32)
                Ex = sbc("Ex", [128, 1024], BF16)
                dn = sbc("dn", [128, 128], F32)
                rc = sbc("rc", [128, 128], F32)
                av = sbc("av", [128, 1024], F32)
                attb = sbc("attb", [128, 1024], BF16)
                a16r = sbc("a16r", [128, 2048], BF16)
                a16c = sbc("a16c", [128, 2048], BF16)
                ap_sb = sbc("ap_sb", [128, 2048], BF16)
                sq_sb = sbc("sq_sb", [128, 2048], BF16)
                s0v = sbc("s0v", [8, 2048], F32)
                s1v = sbc("s1v", [8, 2048], F32)
                st8 = sbc("st8", [8, 2048], F32)
                s1b = sbc("s1b", [8, 2048], BF16)
                st8b = sbc("st8b", [8, 2048], BF16)
                rbc = sbc("rbc", [128, 2048], BF16)
                mbc = sbc("mbc", [128, 2048], BF16)
                fin = sbc("fin", [128, 2048], BF16)
                b2r = sbc("b2r", [128, 1024], BF16)
                pA = psc("pA", [128, 512], F32)
                pB2 = psc("pB2", [128, 512], F32)
                ptq = psc("ptq", [128, 512], BF16)
                pP = psc("pP", [128, 512], F32)
                s0p = psc("s0p", [8, 512], F32)
                s1p = psc("s1p", [8, 512], F32)
                pE = psc("pE", [128, 512], F32)

                def VV(tile, pitch, off, dims):
                    from concourse.ap import AP as _AP
                    return _AP(tile.ap().tensor, off, [[pitch, 128]] + dims)

                for img in (1, 0):
                    for slab in range(8):
                        if img == 0:
                            @p.on("sync")
                            def _(e, slab=slab):
                                p.w(e, "te", "ve")
                                e.dma_start(out=fslab[:].rearrange("p (k r) -> p k r", r=512),
                                            in_=ln_hbm[:, slab * 512:(slab + 1) * 512].rearrange(
                                                "(k p) r -> p k r", p=128)).then_inc(sems["ld"], 16)
                                p.inc("ld", 16)
                        else:
                            for rb in range(4):
                                B = slab * 4 + rb
                                @p.on("sync")
                                def _(e, B=B, rb=rb):
                                    p.w(e, "te", "ve")
                                    from concourse.ap import AP as _AP
                                    for ghb in range(8):
                                        sap = _AP(lntok.tensor, ((128 * ghb) * 512 + 16 * B) * 16,
                                                  [[16, 16], [512 * 16, 128], [1, 16]])
                                        dap = _AP(rm2.ap().tensor, ghb * 2048,
                                                  [[8 * 2048, 16], [16, 128], [1, 16]])
                                        e.dma_start(out=dap, in_=sap).then_inc(sems["ld"], 16)
                                        p.inc("ld", 16)

                                for cg in range(4):
                                    @p.on("tensor")
                                    def _(e, rb=rb, cg=cg):
                                        p.w(e, "ld", "ve")
                                        for cc in range(4):
                                            C = cg * 4 + cc
                                            e.transpose(ptq[:, cc * 128:(cc + 1) * 128],
                                                        rm2[:, C * 128:(C + 1) * 128],
                                                        ident[:]).then_inc(sems["te"], 1)
                                            p.inc("te", 1)

                                    @p.on("vector")
                                    def _(e, rb=rb, cg=cg):
                                        p.w(e, "te")
                                        for cc in range(4):
                                            C = cg * 4 + cc
                                            e.tensor_copy(fslab[:, C * 512 + rb * 128:C * 512 + rb * 128 + 128],
                                                          ptq[:, cc * 128:(cc + 1) * 128])
                                        e.engine_nop().then_inc(sems["ve"], 1)
                                        p.inc("ve", 1)

                        for C in range(DC):
                            @p.on("tensor")
                            def _(e, img=img, C=C):
                                p.w(e, "ld", "ve", "gd")
                                e.matmul(pA[:], mmA[img][:], fslab[:, C * 512:(C + 1) * 512],
                                         start=True, stop=True).then_inc(sems["te"], 1)
                                e.matmul(pB2[:], mmB[img][:], fslab[:, C * 512:(C + 1) * 512],
                                         start=True, stop=True).then_inc(sems["te"], 1)
                                p.inc("te", 2)

                            @p.on("vector")
                            def _(e, C=C):
                                p.w(e, "te")
                                e.tensor_copy(SQK[:, C * 512:(C + 1) * 512], pA[:])
                                e.tensor_copy(SVL[:, C * 512:(C + 1) * 512], pB2[:]).then_inc(sems["ve"], 1)
                                p.inc("ve", 1)

                        for rb in range(4):
                            B = slab * 4 + rb
                            for src_t, dst_t in ((SQK, RM0), (SVL, RM1)):
                                for cg in range(4):
                                    @p.on("tensor")
                                    def _(e, src_t=src_t, rb=rb, cg=cg):
                                        p.w(e, "ve")
                                        for cc in range(4):
                                            C = cg * 4 + cc
                                            e.transpose(ptq[:, cc * 128:(cc + 1) * 128],
                                                        src_t[:, C * 512 + rb * 128:C * 512 + rb * 128 + 128],
                                                        ident[:]).then_inc(sems["te"], 1)
                                            p.inc("te", 1)

                                    @p.on("vector")
                                    def _(e, dst_t=dst_t, cg=cg):
                                        p.w(e, "te")
                                        e.tensor_copy(dst_t[:, cg * 512:(cg + 1) * 512], ptq[:]).then_inc(sems["ve"], 1)
                                        p.inc("ve", 1)

                            @p.on("vector")
                            def _(e, img=img):
                                p.w(e, "ve")
                                for gq in range(8):
                                    qv = VV(RM0, 2048, gq * 8, [[128, 16], [0, 8], [1, 8]])
                                    kv = VV(RM0, 2048, 64, [[128, 16], [8, 8], [1, 8]])
                                    spv = VV(sp, 8192, gq * 64, [[512, 16], [8, 8], [1, 8]])
                                    e.tensor_tensor(spv, qv, kv, ALU.mult).then_inc(sems["ve"], 1)
                                    p.inc("ve", 1)
                                p.w(e, "ve")
                                e.tensor_reduce(Ssc[:], sp[:].rearrange("p (a d) -> p a d", d=8),
                                                mybir.AxisListType.X, ALU.add).then_inc(sems["ve"], 1)
                                p.inc("ve", 1)
                                p.w(e, "ve")
                                e.reduce_max(Mx[:], Ssc[:].rearrange("p (a d) -> p a d", d=8),
                                             mybir.AxisListType.X).then_inc(sems["ve"], 1)
                                p.inc("ve", 1)
                                p.w(e, "ve")
                                mbv = VV(Mx, 128, 0, [[8, 16], [1, 8], [0, 8]])
                                e.tensor_sub(Ssc[:].rearrange("p (a b c) -> p a b c", a=16, b=8, c=8),
                                             Ssc[:].rearrange("p (a b c) -> p a b c", a=16, b=8, c=8),
                                             mbv).then_inc(sems["ve"], 1)
                                p.inc("ve", 1)

                            @p.on("scalar")
                            def _(e):
                                p.w(e, "ve")
                                e.activation(Ex[:], Ssc[:], AF.Exp, scale=qscv[:]).then_inc(sems["se"], 1)
                                p.inc("se", 1)

                            @p.on("vector")
                            def _(e, img=img):
                                p.w(e, "se")
                                e.tensor_reduce(dn[:], Ex[:].rearrange("p (a d) -> p a d", d=8),
                                                mybir.AxisListType.X, ALU.add).then_inc(sems["ve"], 1)
                                p.inc("ve", 1)
                                p.w(e, "ve")
                                e.reciprocal(rc[:], dn[:]).then_inc(sems["ve"], 1)
                                p.inc("ve", 1)
                                for gq in range(8):
                                    ev = VV(Ex, 1024, gq * 8, [[64, 16], [0, 8], [1, 8]])
                                    vv = VV(RM1, 2048, 0, [[128, 16], [1, 8], [8, 8]])
                                    spv = VV(sp, 8192, gq * 64, [[512, 16], [8, 8], [1, 8]])
                                    e.tensor_tensor(spv, ev, vv, ALU.mult).then_inc(sems["ve"], 1)
                                    p.inc("ve", 1)
                                p.w(e, "ve")
                                e.tensor_reduce(av[:], sp[:].rearrange("p (a d) -> p a d", d=8),
                                                mybir.AxisListType.X, ALU.add).then_inc(sems["ve"], 1)
                                p.inc("ve", 1)
                                p.w(e, "ve", "st")
                                rcv = VV(rc, 128, 0, [[8, 16], [1, 8], [0, 8]])
                                e.tensor_tensor(attb[:].rearrange("p (a b c) -> p a b c", a=16, b=8, c=8),
                                                av[:].rearrange("p (a b c) -> p a b c", a=16, b=8, c=8),
                                                rcv, ALU.mult).then_inc(sems["ve"], 1)
                                p.inc("ve", 1)
                                p.w(e, "ve")
                                lpv = VV(RM1, 2048, 64, [[128, 16], [8, 8], [1, 8]])
                                e.tensor_add(attb[:].rearrange("p (a b) -> p a b", a=16, b=64),
                                             attb[:].rearrange("p (a b) -> p a b", a=16, b=64),
                                             lpv).then_inc(sems["ve"], 1)
                                p.inc("ve", 1)
                                p.w(e, "ve")
                                e.tensor_add(attb[:], attb[:], cbbc[img][:]).then_inc(sems["ve"], 1)
                                p.inc("ve", 1)

                            if img == 1:
                                @p.on("sync")
                                def _(e, B=B):
                                    p.w(e, "ve")
                                    from concourse.ap import AP as _AP
                                    for ghb in range(8):
                                        dap = _AP(b2tok.tensor, ((128 * ghb) * 512 + 16 * B) * 8,
                                                  [[8, 16], [512 * 8, 128], [1, 8]])
                                        sap = _AP(attb.ap().tensor, ghb * 1024,
                                                  [[8 * 1024, 16], [8, 128], [1, 8]])
                                        e.dma_start(out=dap, in_=sap).then_inc(sems["st"], 16)
                                        p.inc("st", 16)
                            else:
                                @p.on("vector")
                                def _(e):
                                    p.w(e, "ve", "gd")
                                    dv = VV(a16r, 2048, 0, [[128, 16], [16, 8], [1, 8]])
                                    e.tensor_copy(dv, attb[:].rearrange("p (a b c) -> p a b c",
                                                                        a=16, b=8, c=8)).then_inc(sems["ve"], 1)
                                    p.inc("ve", 1)

                                @p.on("gpsimd")
                                def _(e, B=B):
                                    p.w(e, "ve")
                                    e.dma_start(out=b2r[:], in_=b2tok[16384 * B:16384 * (B + 1), :].rearrange(
                                        "(r q) c -> r (q c)", r=128)).then_inc(sems["gd"], 16)
                                    p.inc("gd", 16)

                                @p.on("vector")
                                def _(e):
                                    p.w(e, "gd")
                                    dv2 = VV(a16r, 2048, 8, [[128, 16], [16, 8], [1, 8]])
                                    e.tensor_copy(dv2, b2r[:].rearrange("p (a b c) -> p a b c",
                                                                        a=16, b=8, c=8)).then_inc(sems["ve"], 1)
                                    p.inc("ve", 1)

                                for cg in range(4):
                                    @p.on("tensor")
                                    def _(e, cg=cg):
                                        p.w(e, "ve", "gd")
                                        for cc in range(4):
                                            C = cg * 4 + cc
                                            e.transpose(ptq[:, cc * 128:(cc + 1) * 128],
                                                        a16r[:, C * 128:(C + 1) * 128],
                                                        ident[:]).then_inc(sems["te"], 1)
                                            p.inc("te", 1)

                                    @p.on("vector")
                                    def _(e, cg=cg):
                                        p.w(e, "te")
                                        e.tensor_copy(a16c[:, cg * 512:(cg + 1) * 512], ptq[:]).then_inc(sems["ve"], 1)
                                        p.inc("ve", 1)

                                for cg in range(4):
                                    @p.on("tensor")
                                    def _(e, cg=cg):
                                        p.w(e, "ve")
                                        for cc in range(4):
                                            C = cg * 4 + cc
                                            e.matmul(pP[:, cc * 128:(cc + 1) * 128], bdp_sb[:],
                                                     a16c[:, C * 128:(C + 1) * 128],
                                                     start=True, stop=True).then_inc(sems["te"], 1)
                                            p.inc("te", 1)

                                    @p.on("vector")
                                    def _(e, cg=cg):
                                        p.w(e, "te")
                                        e.tensor_single_scalar(ap_sb[:, cg * 512:(cg + 1) * 512], pP[:],
                                                               pbv[:], ALU.add).then_inc(sems["ve"], 1)
                                        p.inc("ve", 1)

                                @p.on("scalar")
                                def _(e):
                                    p.w(e, "ve")
                                    e.activation(sq_sb[:], ap_sb[:], AF.Square).then_inc(sems["se"], 1)
                                    p.inc("se", 1)

                                for cg in range(4):
                                    @p.on("tensor")
                                    def _(e, cg=cg):
                                        p.w(e, "se", "ve")
                                        for cc in range(4):
                                            C = cg * 4 + cc
                                            e.matmul(s0p[:, cc * 128:(cc + 1) * 128], sel8_sb[:],
                                                     ap_sb[:, C * 128:(C + 1) * 128],
                                                     start=True, stop=True)
                                            e.matmul(s1p[:, cc * 128:(cc + 1) * 128], sel8_sb[:],
                                                     sq_sb[:, C * 128:(C + 1) * 128],
                                                     start=True, stop=True).then_inc(sems["te"], 1)
                                            p.inc("te", 1)

                                    @p.on("vector")
                                    def _(e, cg=cg):
                                        p.w(e, "te")
                                        e.tensor_copy(s0v[:, cg * 512:(cg + 1) * 512], s0p[:])
                                        e.tensor_copy(s1v[:, cg * 512:(cg + 1) * 512], s1p[:]).then_inc(sems["ve"], 1)
                                        p.inc("ve", 1)

                                @p.on("vector")
                                def _(e):
                                    p.w(e, "ve")
                                    e.tensor_scalar_mul(s0v[:], s0v[:], 1.0 / 16).then_inc(sems["ve"], 1)
                                    p.inc("ve", 1)
                                    e.tensor_scalar_mul(s1v[:], s1v[:], 1.0 / 16).then_inc(sems["ve"], 1)
                                    p.inc("ve", 1)
                                    p.w(e, "ve")
                                    e.tensor_mul(st8[:], s0v[:], s0v[:]).then_inc(sems["ve"], 1)
                                    p.inc("ve", 1)
                                    p.w(e, "ve")
                                    e.tensor_sub(s1v[:], s1v[:], st8[:]).then_inc(sems["ve"], 1)
                                    p.inc("ve", 1)

                                @p.on("scalar")
                                def _(e):
                                    p.w(e, "ve")
                                    e.activation(st8[:], s1v[:], AF.Ln, bias=epsv[:]).then_inc(sems["se"], 1)
                                    p.inc("se", 1)
                                    p.w(e, "se")
                                    e.activation(s1v[:], st8[:], AF.Exp, scale=mhalf[:]).then_inc(sems["se"], 1)
                                    p.inc("se", 1)

                                @p.on("vector")
                                def _(e):
                                    p.w(e, "se")
                                    e.tensor_mul(st8[:], s0v[:], s1v[:]).then_inc(sems["ve"], 1)
                                    p.inc("ve", 1)
                                    p.w(e, "ve")
                                    e.tensor_copy(s1b[:], s1v[:]).then_inc(sems["ve"], 1)
                                    p.inc("ve", 1)
                                    e.tensor_copy(st8b[:], st8[:]).then_inc(sems["ve"], 1)
                                    p.inc("ve", 1)

                                for src8, dstb in ((s1b, rbc), (st8b, mbc)):
                                    for t4 in range(4):
                                        @p.on("tensor")
                                        def _(e, src8=src8, t4=t4):
                                            p.w(e, "ve")
                                            e.matmul(pE[:], exp8_sb[:], src8[:, t4 * 512:(t4 + 1) * 512],
                                                     start=True, stop=True).then_inc(sems["te"], 1)
                                            p.inc("te", 1)

                                        @p.on("vector")
                                        def _(e, dstb=dstb, t4=t4):
                                            p.w(e, "te")
                                            e.tensor_copy(dstb[:, t4 * 512:(t4 + 1) * 512], pE[:]).then_inc(sems["ve"], 1)
                                            p.inc("ve", 1)

                                @p.on("vector")
                                def _(e, rb=rb, B=B):
                                    p.w(e, "ve", "st")
                                    e.tensor_mul(ap_sb[:], ap_sb[:], rbc[:]).then_inc(sems["ve"], 1)
                                    p.inc("ve", 1)
                                    p.w(e, "ve")
                                    e.tensor_sub(ap_sb[:], ap_sb[:], mbc[:]).then_inc(sems["ve"], 1)
                                    p.inc("ve", 1)
                                    p.w(e, "ve")
                                    e.tensor_scalar(ap_sb[:], ap_sb[:], lnwv[:], lnbv[:],
                                                    ALU.mult, ALU.add).then_inc(sems["ve"], 1)
                                    p.inc("ve", 1)
                                    p.w(e, "ve")
                                    tokv = VV(fslab, 16 * 512, rb * 128, [[512, 16], [1, 128]])
                                    e.tensor_add(fin[:].rearrange("p (a b) -> p a b", a=16, b=128),
                                                 ap_sb[:].rearrange("p (a b) -> p a b", a=16, b=128),
                                                 tokv).then_inc(sems["ve"], 1)
                                    p.inc("ve", 1)

                                @p.on("scalar")
                                def _(e, B=B):
                                    p.w(e, "ve")
                                    from concourse.ap import AP as _AP
                                    dap = _AP(attout.tensor, B * 128, [[4096, 128], [128 * 4096, 16], [1, 128]])
                                    e.dma_start(out=dap, in_=fin[:].rearrange("p (a b) -> p a b", a=16, b=128)).then_inc(sems["st"], 16)
                                    p.inc("st", 16)

                    p.barrier()

            @p.on("gpsimd")
            def _(e):
                p.w(e, "st")

    return nc


_prog_cache = {}


def _get_prog(name, builder, *args):
    key = (name,) + args
    if key not in _prog_cache:
        _prog_cache[key] = builder(*args)
    return _prog_cache[key]


def run_l0ffn(xT_cores, inputs, addend_cores=None):
    nc = _get_prog("l0ffn", build_l0ffn, addend_cores is not None)
    wgT = np.ascontiguousarray(inputs["L0_ffn_gate_w"].T)
    wuT = np.ascontiguousarray(inputs["L0_ffn_up_w"].T)
    wdT = np.ascontiguousarray(inputs["L0_ffn_down_w"].T)
    sg = np.ascontiguousarray(inputs["L0_ffn_gate_scale"].reshape(64, 128).T)
    su = np.ascontiguousarray(inputs["L0_ffn_up_scale"].reshape(64, 128).T)
    sd = np.ascontiguousarray(inputs["L0_ffn_down_scale"].reshape(16, 128).T)
    maps = []
    for ci in range(8):
        m = {"x": xT_cores[ci], "wgT": wgT, "wuT": wuT, "wdT": wdT,
             "sg": sg, "su": su, "sd": sd}
        if addend_cores is not None:
            m["addend"] = addend_cores[ci]
        maps.append(m)
    res = run_bass_kernel_spmd(nc, maps, core_ids=list(range(8)))
    return [res.results[i]["xout"] for i in range(8)]


def split_cores(x):
    out = []
    for ci in range(8):
        b, q = divmod(ci, 4)
        rows = x[b, q * TPC:(q + 1) * TPC, :]
        out.append(np.ascontiguousarray(rows.T))
    return out


def merge_cores(xT_cores):
    x = np.empty((2, 16384, D), np.float32)
    for ci in range(8):
        b, q = divmod(ci, 4)
        x[b, q * TPC:(q + 1) * TPC, :] = xT_cores[ci].T
    return x


def run_prescan(xT_cores, inputs):
    nc = _get_prog("prescan", build_prescan)
    winT = np.ascontiguousarray(inputs["L1_attn_W_in"].T)
    maps = [{"x": xT_cores[ci], "winT": winT} for ci in range(8)]
    res = run_bass_kernel_spmd(nc, maps, core_ids=list(range(8)))
    return [res.results[i]["xzout"] for i in range(8)]


def _mamba_slices(inputs, j):
    cw = inputs["L1_attn_conv_w"][j * 512:(j + 1) * 512]
    cb = inputs["L1_attn_conv_b"][j * 512:(j + 1) * 512]
    wx = inputs["L1_attn_W_x"][:, j * 512:(j + 1) * 512]
    wdt = inputs["L1_attn_W_dt"][j * 512:(j + 1) * 512]
    dtb = inputs["L1_attn_dt_bias"][j * 512:(j + 1) * 512]
    A = inputs["L1_attn_A"][j * 512:(j + 1) * 512]
    Dv = inputs["L1_attn_D"][j * 512:(j + 1) * 512]
    return {
        "convw": np.ascontiguousarray(cw.reshape(4, 128, 4).transpose(1, 0, 2).reshape(128, 16)),
        "convb": np.ascontiguousarray(cb.reshape(4, 128).T),
        "wxT": np.ascontiguousarray(wx.T),
        "wdtT": np.ascontiguousarray(wdt.T),
        "dtb": np.ascontiguousarray(dtb.reshape(4, 128).T),
        "Asl": np.ascontiguousarray(A.reshape(4, 128, 16).transpose(1, 0, 2).reshape(128, 64)),
        "Dsl": np.ascontiguousarray(Dv.reshape(4, 128).T),
    }


def run_mambapre(xz_cores, inputs):
    nc = _get_prog("mambapre", build_mambapre)
    maps = []
    for j in range(8):
        sl = _mamba_slices(inputs, j)
        xzx = np.concatenate([xz_cores[c][j * 512:(j + 1) * 512, :] for c in range(8)], axis=1)
        maps.append({"xzx": np.ascontiguousarray(xzx), "convw": sl["convw"],
                     "convb": sl["convb"], "wxT": sl["wxT"]})
    res = run_bass_kernel_spmd(nc, maps, core_ids=list(range(8)))
    return ([res.results[i]["xcout"] for i in range(8)],
            [res.results[i]["pssm"] for i in range(8)])


def run_mambascan(xc_cores, pssm_cores, xz_cores, inputs):
    nc = _get_prog("mambascan", build_mambascan)
    pall = np.ascontiguousarray(np.concatenate(pssm_cores, axis=0))
    maps = []
    for j in range(8):
        sl = _mamba_slices(inputs, j)
        szl = np.concatenate([xz_cores[c][4096 + j * 512:4096 + (j + 1) * 512, :]
                              for c in range(8)], axis=1)
        maps.append({"xc": xc_cores[j], "sz": np.ascontiguousarray(szl), "pssm": pall,
                     "wdtT": sl["wdtT"], "dtb": sl["dtb"], "Asl": sl["Asl"], "Dsl": sl["Dsl"]})
    res = run_bass_kernel_spmd(nc, maps, core_ids=list(range(8)))
    return [res.results[i]["ygout"] for i in range(8)]


def run_l1ffn(xT_cores, inputs, yg_cores=None):
    nc = _get_prog("l1ffn", build_l1ffn, yg_cores is not None)
    wfT = np.ascontiguousarray(inputs["L1_ffn_w_fc"].T)
    wpT = np.ascontiguousarray(inputs["L1_ffn_w_proj"].T)
    maps = []
    for ci in range(8):
        m = {"x": xT_cores[ci], "wfT": wfT, "wpT": wpT}
        if yg_cores is not None:
            b, q = divmod(ci, 4)
            tsl = slice(b * 16384 + q * TPC, b * 16384 + (q + 1) * TPC)
            ygT = np.concatenate([yg_cores[j][:, tsl] for j in range(8)], axis=0)
            m["ygT"] = np.ascontiguousarray(ygT)
            m["woT"] = np.ascontiguousarray(inputs["L1_attn_W_out"].T)
        maps.append(m)
    res = run_bass_kernel_spmd(nc, maps, core_ids=list(range(8)))
    return [res.results[i]["xout"] for i in range(8)]


def _attn_consts(inputs):
    qkv = inputs["L0_attn_qkv_w"].astype(np.float32)
    cw = [inputs["L0_attn_conv_w0"], inputs["L0_attn_conv_w1"]]
    cb = [inputs["L0_attn_conv_b0"], inputs["L0_attn_conv_b1"]]
    pw = inputs["L0_attn_proj_w"].astype(np.float32)
    pb = inputs["L0_attn_proj_b"].astype(np.float32)
    lnw = inputs["L0_attn_ln_w"].astype(np.float32)
    lnb = inputs["L0_attn_ln_b"].astype(np.float32)
    out = {}
    for br in range(2):
        bdqk = np.zeros((128, 128), np.float32)
        bdv = np.zeros((128, 64), np.float32)
        bdvs = np.zeros((128, 3 * 64), np.float32)
        for g in range(8):
            rows = slice(16 * g, 16 * g + 16)
            bdqk[rows, 8 * g:8 * g + 8] = qkv[:, 8 * br:8 * br + 8]
            bdqk[rows, 64 + 8 * g:64 + 8 * g + 8] = qkv[:, 16 + 8 * br:16 + 8 * br + 8]
            bdv[rows, 8 * g:8 * g + 8] = qkv[:, 32 + 8 * br:32 + 8 * br + 8]
        for di, dd in enumerate((-1, 0, 1)):
            for g in range(8):
                gs = g + dd
                if 0 <= gs < 8:
                    bdvs[16 * gs:16 * gs + 16, di * 64 + 8 * g:di * 64 + 8 * g + 8] = \
                        qkv[:, 32 + 8 * br:32 + 8 * br + 8]
        w33 = cw[br].reshape(8, 3, 3)
        tapvals = w33[:, 1, :] if br == 0 else w33[:, :, 1]
        taps = np.concatenate([np.tile(tapvals[:, di], 8) for di in range(3)])[None, :]
        out[f"bdqk{br}"] = bdqk
        out[f"bdv{br}"] = bdv
        out[f"bdvs{br}"] = bdvs
        out[f"taps{br}"] = np.ascontiguousarray(taps.astype(np.float32))
        out[f"cbr{br}"] = np.ascontiguousarray(np.tile(cb[br].astype(np.float32), 128)[None, :])
    bdproj = np.zeros((128, 128), np.float32)
    for g in range(8):
        bdproj[16 * g:16 * g + 16, 16 * g:16 * g + 16] = pw
    out["bdproj"] = bdproj
    out["pb_slot"] = np.ascontiguousarray(np.tile(pb, 8).reshape(128, 1))
    out["lnw_slot"] = np.ascontiguousarray(np.tile(lnw, 8).reshape(128, 1))
    out["lnb_slot"] = np.ascontiguousarray(np.tile(lnb, 8).reshape(128, 1))
    sel8 = np.zeros((128, 8), np.float32)
    for pp in range(128):
        sel8[pp, pp // 16] = 1.0
    out["sel8"] = sel8
    out["exp8"] = np.ascontiguousarray(sel8.T)
    return out


def run_attn(xT_cores, inputs):
    nc = _get_prog("attn", build_attn)
    consts = _attn_consts(inputs)
    maps = [dict(consts, x=xT_cores[ci]) for ci in range(8)]
    res = run_bass_kernel_spmd(nc, maps, core_ids=list(range(8)))
    return [res.results[i]["attout"] for i in range(8)]


def attn_addend(att_cores):
    addT = []
    for b in range(2):
        tok = np.concatenate(
            [att_cores[4 * b + q].T.reshape(TPC * 128, 16) for q in range(4)], axis=0)
        add_rows = np.ascontiguousarray(tok.T).reshape(16384, 2048)
        for q in range(4):
            addT.append(np.ascontiguousarray(
                add_rows[q * TPC:(q + 1) * TPC, :].T.astype(np.float32)))
    return [addT[0], addT[1], addT[2], addT[3], addT[4], addT[5], addT[6], addT[7]]


def kernel(**inputs):
    x = np.asarray(inputs["L0_attn_x"], np.float32)
    xT = split_cores(x)
    x1 = run_l0ffn(xT, inputs)
    att = run_attn(x1, inputs)
    addend = attn_addend(att)
    x3 = run_l0ffn(x1, inputs, addend_cores=addend)
    x4 = run_l1ffn(x3, inputs)
    xz = run_prescan(x4, inputs)
    xc, pssm = run_mambapre(xz, inputs)
    yg = run_mambascan(xc, pssm, xz, inputs)
    x6 = run_l1ffn(x4, inputs, yg_cores=yg)
    return merge_cores(x6)
